# version 03
# baseline (speedup 1.0000x reference)
import numpy as np
import ml_dtypes
import contextlib
import concourse.bass as bass
import concourse.mybir as mybir
from concourse.bass_utils import run_bass_kernel_spmd

F32 = mybir.dt.float32
BF16 = mybir.dt.bfloat16
I8 = mybir.dt.int8
AF = mybir.ActivationFunctionType
ALU = mybir.AluOpType
AX = mybir.AxisListType

P = 128
TOK = 1024
TT = 8
D = 2048
KD = 16
H = 8192
KH = 64
OCA = 16
OCB = 8
INV127 = 1.0 / 127.0
INV2048 = 1.0 / 2048.0
INV2_24 = 1.0 / float(1 << 24)
EPS = 1e-5


def build(phases=None, dbg=None):
    nc = bass.Bass("TRN2", target_bir_lowering=False)

    u_in = nc.dram_tensor("u", [TOK, D], F32, kind="ExternalInput").ap()
    wg_in = nc.dram_tensor("wg", [H, D], F32, kind="ExternalInput").ap()
    wu_in = nc.dram_tensor("wu", [H, D], F32, kind="ExternalInput").ap()
    wd_in = nc.dram_tensor("wd", [D, H], F32, kind="ExternalInput").ap()
    a_fl = nc.dram_tensor("a_fl", [1, 32], F32, kind="ExternalInput").ap()
    wbr_in = nc.dram_tensor("wbr", [D, 64], F32, kind="ExternalInput").ap()
    wbs_in = nc.dram_tensor("wbs", [64, 256], F32, kind="ExternalInput").ap()
    wcr_in = nc.dram_tensor("wcr", [256, 64], F32, kind="ExternalInput").ap()
    wcm_in = nc.dram_tensor("wcm", [64, D], F32, kind="ExternalInput").ap()
    dv_in = nc.dram_tensor("dv", [1, D], F32, kind="ExternalInput").ap()
    idb_in = nc.dram_tensor("idb", [P, P], BF16, kind="ExternalInput").ap()
    idf_in = nc.dram_tensor("idf", [P, P], F32, kind="ExternalInput").ap()
    onc_in = nc.dram_tensor("onc", [P, 1], F32, kind="ExternalInput").ap()
    onr_in = nc.dram_tensor("onr", [1, P], F32, kind="ExternalInput").ap()

    out_d = nc.dram_tensor("out", [TOK, D], F32, kind="ExternalOutput").ap()

    wgT_h = nc.dram_tensor("wgT_h", [D, H], BF16, kind="Internal").ap()
    wuT_h = nc.dram_tensor("wuT_h", [D, H], BF16, kind="Internal").ap()
    wdT_h = nc.dram_tensor("wdT_h", [H, D], BF16, kind="Internal").ap()
    h_h = nc.dram_tensor("h_h", [TOK, H], F32, kind="Internal").ap()
    x1_h = nc.dram_tensor("x1_h", [TOK, D], F32, kind="Internal").ap()
    x2_h = nc.dram_tensor("x2_h", [TOK, D], F32, kind="Internal").ap()

    RgT = wgT_h.rearrange("(k p) o -> p k o", p=P)
    RuT = wuT_h.rearrange("(k p) o -> p k o", p=P)
    RdT = wdT_h.rearrange("(k p) o -> p k o", p=P)

    es = contextlib.ExitStack()
    sb = lambda name, shp, dt: es.enter_context(nc.sbuf_tensor(name, shp, dt))

    arena = sb("arena", [P, 32768], F32)
    AFp = arena.ap()
    ABp = AFp.bitcast(BF16)

    def fa(b0, b1):
        return AFp[:, b0 // 4: b1 // 4]

    def fb(b0, b1):
        return ABp[:, b0 // 2: b1 // 2]

    xqT3 = fb(0, 32768).rearrange("p (k t) -> p k t", k=KD)
    slab3 = [fb(32768 + i * 16384, 32768 + (i + 1) * 16384)
             .rearrange("p (k c) -> p k c", k=KD) for i in range(4)]
    hqT3 = fb(0, 65536).rearrange("p (k c) -> p k c", k=KH)
    wdsl3 = [fb(65536 + i * 16384, 65536 + (i + 1) * 16384)
             .rearrange("p (k c) -> p k c", k=32) for i in range(2)]
    wtmp = [fa(98304 + i * 8192, 98304 + (i + 1) * 8192) for i in range(2)]
    sqscr = fa(114688, 122880)
    wqb_v = fb(114688, 122880)
    dstg_v = fb(122880, 131072)
    xcT3 = fa(0, 65536).rearrange("p (k t) -> p k t", k=KD)
    dbc = fa(65536, 73728)
    wcmv = fa(73728, 81920)
    t1Tv = fa(81920, 86016)

    xt = sb("xt", [P, 2 * D], F32)
    wtile = sb("wtile", [P, 2 * D], F32)
    xqi8 = sb("xqi8", [P, 2 * D], I8)
    xqb = sb("xqb", [P, 2 * D], BF16)
    wqi8 = sb("wqi8", [P, 2 * D], I8)
    hgb = sb("hgb", [P, 2 * 512], F32)
    hob = sb("hob", [P, 2 * 512], F32)
    yob = sb("yob", [P, 2 * 512], F32)
    rsb = sb("rsb", [P, 2 * 512], F32)
    wleft = sb("wleft", [P, KD * 64], F32)
    redc = sb("redc", [P, 3 * 64], F32)
    identb = sb("identb", [P, P], BF16)
    identf = sb("identf", [P, P], F32)
    onescol = sb("onescol", [P, 1], F32)
    onesrow = sb("onesrow", [1, P], F32)
    a_bc = sb("a_bc", [64, 32], F32)
    wbs_a = sb("wbs_a", [64, 256], F32)
    wbs_b = sb("wbs_b", [64, 256], F32)
    btmp = sb("btmp", [64, 128], F32)
    wbsmT = sb("wbsmT", [P, P], F32)
    wmid = sb("wmid", [64, 64], F32)
    wbrT1 = sb("wbrT1", [64, P], F32)
    wbr_sb = sb("wbr_sb", [P, KD * 64], F32)
    wcr_sb = sb("wcr_sb", [P, 2 * 64], F32)
    totsb = sb("totsb", [1, 4], F32)

    def stat(name):
        return sb(name, [P, 8], F32)

    sumc = stat("sumc"); xmaxc = stat("xmaxc"); xminc = stat("xminc")
    ssqc = stat("ssqc"); mcol = stat("mcol"); nmcol = stat("nmcol")
    vc = stat("vc"); sdc = stat("sdc"); rstdc = stat("rstdc")
    amax1 = stat("amax1"); amax2 = stat("amax2"); amaxc = stat("amaxc")
    sc_ = stat("sc_"); rsc = stat("rsc"); qqc = stat("qqc"); nmqc = stat("nmqc")
    sgat = stat("sgat"); supc = stat("supc")
    hmaxc = stat("hmaxc"); s2c = stat("s2c"); rs2c = stat("rs2c"); scoc = stat("scoc")
    tmp1c = stat("tmp1c")
    wsraw = sb("wsraw", [P, 3], F32)
    wsb = sb("wsb", [P, 3], F32)
    rwsb = sb("rwsb", [P, 3], F32)

    psA = [es.enter_context(nc.psum_tensor(f"psA{i}", [P, 512], F32)) for i in range(4)]
    psB = [es.enter_context(nc.psum_tensor(f"psB{i}", [P, 512], F32)) for i in range(4)]
    psAb = [t.ap().bitcast(BF16) for t in psA]
    psBb = [t.ap().bitcast(BF16) for t in psB]
    psA3 = [t.ap().bitcast(BF16).rearrange("p (e c) -> p e c", e=8) for t in psA]
    psB3 = [t.ap().bitcast(BF16).rearrange("p (e c) -> p e c", e=8) for t in psB]
    psAf3 = [t.ap().rearrange("p (e c) -> p e c", e=4) for t in psA]

    sem_names = ["SM", "LDW0", "LDW1", "XT0", "XT1", "CW0", "CW1", "CX0", "CX1",
                 "SL0", "SL1", "SL2", "SL3", "WD0", "WD1", "HS0", "HS1",
                 "RS0", "RS1", "YW0", "YW1", "SPA0", "SPA1", "SPD0", "SPD1",
                 "V", "S", "TE"]
    S = {n: es.enter_context(nc.semaphore(n)) for n in sem_names}
    C = {n: 0 for n in sem_names}

    prog = {k: [] for k in ("sync", "scalar", "vector", "tensor", "gpsimd")}
    marks = []

    def mark(label=None):
        marks.append(({k: len(v) for k, v in prog.items()}, dict(C), label))

    def p(eng, fn):
        prog[eng].append(fn)

    def wait(eng, name, val=None):
        v = C[name] if val is None else val
        p(eng, lambda q, s=S[name], vv=v: q.wait_ge(s, vv))

    def dma(eng, out, in_, sem):
        p(eng, lambda q, o=out, i=in_, s=S[sem]: q.dma_start(out=o, in_=i).then_inc(s, 16))
        C[sem] += 16

    def op(eng, fn, sem=None):
        if sem is None:
            p(eng, fn)
        else:
            p(eng, lambda q, f=fn, s=S[sem]: f(q).then_inc(s, 1))
            C[sem] += 1

    def dr():
        p("vector", lambda q: q.drain())

    def vs_fence():
        op("vector", lambda q: q.tensor_copy(totsb.ap()[0:1, 3:4], totsb.ap()[0:1, 2:3]),
           sem="V")
        wait("scalar", "V", C["V"])
        op("scalar", lambda q: q.copy(totsb.ap()[0:1, 2:3], totsb.ap()[0:1, 3:4]), sem="S")
        wait("vector", "S", C["S"])

    def recip(dst_fn, src_desc):
        vs_fence()
        op("vector", dst_fn, sem="V")
        vs_fence()

    def te_settle_v():
        wait("scalar", "TE", C["TE"])
        op("scalar", lambda q: q.copy(totsb.ap()[0:1, 1:2], totsb.ap()[0:1, 0:1]), sem="S")
        wait("vector", "S", C["S"])

    def te_settle_s():
        wait("vector", "TE", C["TE"])
        op("vector", lambda q: q.tensor_copy(totsb.ap()[0:1, 1:2], totsb.ap()[0:1, 0:1]),
           sem="V")
        wait("scalar", "V", C["V"])

    dma("sync", identb[:], idb_in, "SM")
    dma("sync", identf[:], idf_in, "SM")
    dma("sync", onescol[:], onc_in, "SM")
    dma("sync", onesrow[:], onr_in, "SM")
    dma("sync", a_bc[:], a_fl.partition_broadcast(64), "SM")
    dma("sync", wbs_a[:], wbs_in, "SM")
    dma("sync", wcr_sb[:, 0:64], wcr_in[0:128, :], "SM")
    dma("sync", wcr_sb[:, 64:128], wcr_in[128:256, :], "SM")
    for k2 in range(KD):
        dma("sync", wbr_sb.ap()[:, k2 * 64:(k2 + 1) * 64],
            wbr_in[k2 * P:(k2 + 1) * P, :], "SM")
    SM_ALL = C["SM"]

    wait("vector", "SM", SM_ALL)
    for lvl in range(8):
        src = (wbs_a if lvl % 2 == 0 else wbs_b).ap()
        dst = (wbs_b if lvl % 2 == 0 else wbs_a).ap()
        pre = 1 << lvl
        post = 128 >> lvl
        sv = src.rearrange("p (pre two post) -> p pre two post", pre=pre, two=2)
        dvv = dst.rearrange("p (pre two post) -> p pre two post", pre=pre, two=2)
        a00 = a_bc.ap()[:, 4 * lvl + 0: 4 * lvl + 1]
        a01 = a_bc.ap()[:, 4 * lvl + 1: 4 * lvl + 2]
        a10 = a_bc.ap()[:, 4 * lvl + 2: 4 * lvl + 3]
        a11 = a_bc.ap()[:, 4 * lvl + 3: 4 * lvl + 4]
        tb = btmp.ap()[:, 0:128].rearrange("p (pre post) -> p pre post", pre=pre)
        last = (lvl == 7)
        dr()
        op("vector", lambda q, o=tb, i=sv, a=a00: q.tensor_scalar(o, i[:, :, 0, :], a, None, ALU.mult))
        dr()
        op("vector", lambda q, o=dvv, i=sv, a=a10, t=tb:
           q.scalar_tensor_tensor(o[:, :, 0, :], i[:, :, 1, :], a, t, ALU.mult, ALU.add))
        dr()
        op("vector", lambda q, o=tb, i=sv, a=a01: q.tensor_scalar(o, i[:, :, 0, :], a, None, ALU.mult))
        dr()
        op("vector", lambda q, o=dvv, i=sv, a=a11, t=tb:
           q.scalar_tensor_tensor(o[:, :, 1, :], i[:, :, 1, :], a, t, ALU.mult, ALU.add),
           sem="V" if last else None)
    V_BFLY = C["V"]
    wbs_fin = wbs_b
    wbs_fin = wbs_a if (8 % 2 == 0) else wbs_b
    wbs_fin = wbs_a

    wait("tensor", "V", V_BFLY)
    op("tensor", lambda q: q.transpose(psB[0].ap()[0:128, 0:64], wbs_fin.ap()[:, 0:128],
                                       identf.ap()[0:64, 0:64]))
    op("tensor", lambda q: q.transpose(psB[0].ap()[0:128, 64:128], wbs_fin.ap()[:, 128:256],
                                       identf.ap()[0:64, 0:64]), sem="TE")
    TE_T1 = C["TE"]
    te_settle_s()
    wait("scalar", "TE", TE_T1)
    op("scalar", lambda q: q.copy(wbsmT[:], psB[0].ap()[0:128, 0:128]), sem="S")
    S_T1 = C["S"]
    wait("tensor", "S", S_T1)
    op("tensor", lambda q: q.matmul(psB[1].ap()[0:64, 0:64], wbsmT.ap()[:, 0:64],
                                    wcr_sb.ap()[:, 0:64], start=True, stop=False))
    op("tensor", lambda q: q.matmul(psB[1].ap()[0:64, 0:64], wbsmT.ap()[:, 64:128],
                                    wcr_sb.ap()[:, 64:128], start=False, stop=True), sem="TE")
    TE_T2 = C["TE"]
    te_settle_s()
    wait("scalar", "TE", TE_T2)
    op("scalar", lambda q: q.copy(wmid[:], psB[1].ap()[0:64, 0:64]), sem="S")
    S_T2 = C["S"]
    wait("tensor", "S", S_T2)
    for k in range(KD):
        wait("tensor", "S", C["S"])
        op("tensor", lambda q, k=k: q.transpose(psB[0].ap()[0:64, 0:128],
                                                wbr_sb.ap()[:, k * 64:(k + 1) * 64],
                                                identf.ap()[0:128, 0:128]), sem="TE")
        te_settle_s()
        wait("scalar", "TE", C["TE"])
        op("scalar", lambda q: q.copy(wbrT1[:], psB[0].ap()[0:64, 0:128]), sem="S")
        wait("tensor", "S", C["S"])
        op("tensor", lambda q: q.matmul(psB[1].ap()[0:128, 0:64], wbrT1.ap()[:],
                                        wmid.ap()[:], start=True, stop=True), sem="TE")
        te_settle_s()
        wait("scalar", "TE", C["TE"])
        op("scalar", lambda q, k=k: q.copy(wleft.ap()[:, k * 64:(k + 1) * 64],
                                           psB[1].ap()[0:128, 0:64]), sem="S")
    S_SETUP = C["S"]
    mark()

    wtile_use = [0, 0]
    wtile_cons = [None, None]

    def wtile_load(src_ap, slot):
        if wtile_cons[slot] is not None:
            nm, vv = wtile_cons[slot]
            wait("sync", nm, vv)
        dma("sync", wtile.ap()[:, slot * D:(slot + 1) * D], src_ap, f"LDW{slot}")

    mats = [("g", wg_in, redc.ap()[:, 0:64]), ("u", wu_in, redc.ap()[:, 64:128]),
            ("d", wd_in, redc.ap()[:, 128:192])]
    for mi, (mn, w_ap, rcol) in enumerate(mats):
        for j in range(64):
            slot = (mi * 64 + j) % 2
            if mn == "d":
                src = w_ap[(j // 4) * P:(j // 4 + 1) * P, (j % 4) * D:(j % 4 + 1) * D]
            else:
                src = w_ap[j * P:(j + 1) * P, :]
            wtile_load(src, slot)
            wait("vector", f"LDW{slot}", C[f"LDW{slot}"])
            op("vector", lambda q, o=rcol, j=j, sl=slot:
               q.tensor_reduce(o[:, j:j + 1], wtile.ap()[:, sl * D:(sl + 1) * D],
                               AX.X, ALU.add, apply_absolute_value=True), sem="V")
            wtile_cons[slot] = ("V", C["V"])
    for mi in range(3):
        rcol = redc.ap()[:, mi * 64:(mi + 1) * 64]
        dr()
        op("vector", lambda q, r=rcol, mi=mi:
           q.tensor_reduce(wsraw.ap()[:, mi:mi + 1], r, AX.X, ALU.add), sem="V")
        wait("tensor", "V", C["V"])
        op("tensor", lambda q, mi=mi: q.matmul(psB[0].ap()[0:1, mi:mi + 1],
                                               wsraw.ap()[:, mi:mi + 1],
                                               onescol.ap()[:], start=True, stop=True), sem="TE")
        te_settle_s()
        wait("scalar", "TE", C["TE"])
        op("scalar", lambda q, mi=mi: q.copy(totsb.ap()[0:1, mi:mi + 1],
                                             psB[0].ap()[0:1, mi:mi + 1]), sem="S")
        wait("tensor", "S", C["S"])
        op("tensor", lambda q, mi=mi: q.matmul(psB[1].ap()[0:128, mi:mi + 1],
                                               onesrow.ap()[:],
                                               totsb.ap()[0:1, mi:mi + 1],
                                               start=True, stop=True), sem="TE")
        te_settle_s()
        wait("scalar", "TE", C["TE"])
        op("scalar", lambda q, mi=mi: q.copy(wsraw.ap()[:, mi:mi + 1],
                                             psB[1].ap()[0:128, mi:mi + 1]), sem="S")
        wait("vector", "S", C["S"])
        op("vector", lambda q, mi=mi: q.tensor_scalar(wsb.ap()[:, mi:mi + 1],
                                                      wsraw.ap()[:, mi:mi + 1],
                                                      INV2_24, EPS, ALU.mult, ALU.add))
        recip(lambda q, mi=mi: q.reciprocal(rwsb.ap()[:, mi:mi + 1],
                                            wsb.ap()[:, mi:mi + 1]), "rws")
    V_WS = C["V"]
    mark()

    xt_use = [0, 0]
    xt_cons = [None, None]

    def xt_load(src_ap, slot, extra_wait=None):
        if xt_cons[slot] is not None:
            nm, vv = xt_cons[slot]
            wait("scalar", nm, vv)
        if extra_wait is not None:
            wait("scalar", extra_wait[0], extra_wait[1])
        dma("scalar", xt.ap()[:, slot * D:(slot + 1) * D], src_ap, f"XT{slot}")

    xq_te = [0, 0]

    def phase_L(src_h, do_quant_stats=True):
        for t in range(TT):
            slot = t % 2
            xt_load(src_h[t * P:(t + 1) * P, :], slot)
            wait("vector", f"XT{slot}", C[f"XT{slot}"])
            xv = xt.ap()[:, slot * D:(slot + 1) * D]
            op("vector", lambda q, xv=xv, t=t:
               q.tensor_reduce(sumc.ap()[:, t:t + 1], xv, AX.X, ALU.add))
            if do_quant_stats:
                op("vector", lambda q, xv=xv, t=t:
                   q.tensor_reduce(xmaxc.ap()[:, t:t + 1], xv, AX.X, ALU.max))
                op("vector", lambda q, xv=xv, t=t:
                   q.tensor_reduce(xminc.ap()[:, t:t + 1], xv, AX.X, ALU.min))
            dr()
            op("vector", lambda q, t=t:
               q.tensor_scalar(mcol.ap()[:, t:t + 1], sumc.ap()[:, t:t + 1],
                               INV2048, None, ALU.mult))
            dr()
            op("vector", lambda q, t=t:
               q.tensor_scalar(nmcol.ap()[:, t:t + 1], mcol.ap()[:, t:t + 1],
                               -1.0, None, ALU.mult), sem="V")
            wait("scalar", "V", C["V"])
            op("scalar", lambda q, xv=xv, t=t:
               q.activation(sqscr, xv, AF.Square, bias=nmcol.ap()[:, t:t + 1],
                            scale=1.0, accum_out=ssqc.ap()[:, t:t + 1]), sem="S")
            xt_cons[slot] = ("S", C["S"])
        wait("vector", "S", C["S"])
        op("vector", lambda q: q.tensor_scalar(vc[:], ssqc[:], INV2048, EPS,
                                               ALU.mult, ALU.add), sem="V")
        wait("scalar", "V", C["V"])
        op("scalar", lambda q: q.activation(sdc[:], vc[:], AF.Sqrt), sem="S")
        wait("vector", "S", C["S"])
        recip(lambda q: q.reciprocal(rstdc[:], sdc[:]), "rstd")
        if do_quant_stats:
            dr()
            op("vector", lambda q: q.tensor_tensor(amax1[:], xmaxc[:], mcol[:], ALU.subtract))
            op("vector", lambda q: q.tensor_tensor(amax2[:], mcol[:], xminc[:], ALU.subtract))
            dr()
            op("vector", lambda q: q.tensor_tensor(amaxc[:], amax1[:], amax2[:], ALU.max))
            dr()
            op("vector", lambda q: q.tensor_tensor(tmp1c[:], amaxc[:], rstdc[:], ALU.mult))
            dr()
            op("vector", lambda q: q.tensor_scalar(sc_[:], tmp1c[:], INV127, EPS,
                                                   ALU.mult, ALU.add))
            recip(lambda q: q.reciprocal(rsc[:], sc_[:]), "rsc")
            op("vector", lambda q: q.tensor_tensor(qqc[:], rstdc[:], rsc[:], ALU.mult))
            dr()
            op("vector", lambda q: q.tensor_tensor(nmqc[:], nmcol[:], qqc[:], ALU.mult),
               sem="V")
        else:
            op("vector", lambda q: q.tensor_copy(tmp1c.ap()[:, 0:1], rstdc.ap()[:, 0:1]),
               sem="V")
        V_STATS = C["V"]

        if not do_quant_stats:
            return

        for t in range(TT):
            slot = t % 2
            xt_load(src_h[t * P:(t + 1) * P, :], slot)
            wait("scalar", f"XT{slot}", C[f"XT{slot}"])
            wait("scalar", "V", V_STATS)
            pp = t % 2
            wait("scalar", f"CX{pp}", C[f"CX{pp}"])
            xv = xt.ap()[:, slot * D:(slot + 1) * D]
            op("scalar", lambda q, xv=xv, t=t, pp=pp:
               q.activation(xqi8.ap()[:, pp * D:(pp + 1) * D], xv, AF.Identity,
                            bias=nmqc.ap()[:, t:t + 1], scale=qqc.ap()[:, t:t + 1]), sem="S")
            xt_cons[slot] = ("S", C["S"])
            wait("gpsimd", "S", C["S"])
            wait("gpsimd", "TE", xq_te[pp])
            dma("gpsimd", xqb.ap()[:, pp * D:(pp + 1) * D],
                xqi8.ap()[:, pp * D:(pp + 1) * D], f"CX{pp}")
            wait("tensor", f"CX{pp}", C[f"CX{pp}"])
            wait("tensor", "V", C["V"])
            wait("tensor", "S", C["S"])
            for kk in range(KD):
                bank = (t % 2) * 2 + kk // 8
                sl = kk % 8
                op("tensor", lambda q, pp=pp, kk=kk, bank=bank, sl=sl:
                   q.transpose(psB3[bank][:, sl, :],
                               xqb.ap()[:, pp * D + kk * P: pp * D + (kk + 1) * P],
                               identb.ap()[:]), sem=("TE" if kk % 8 == 7 else None))
            xq_te[pp] = C["TE"]
            te_settle_v()
            wait("vector", "TE", C["TE"])
            for half in range(2):
                bank = (t % 2) * 2 + half
                op("vector", lambda q, bank=bank, t=t, half=half:
                   q.tensor_copy(xqT3[:, half * 8:(half + 1) * 8, t * P:(t + 1) * P],
                                 psB3[bank][:, :, :]), sem=("V" if half == 1 else None))

    pipe_idx = [0]
    wq_te = [0, 0]
    pipe_v_slab = {}
    slab_guard = {}
    dstg_guard = [0, 0]

    def pipe_piece(mn, w_src_ap, dest_kind, dest, r, mi):
        gi = pipe_idx[0]; pipe_idx[0] += 1
        slot = gi % 2
        wtile_load(w_src_ap, slot)
        wait("scalar", f"LDW{slot}", C[f"LDW{slot}"])
        wait("scalar", "V", C["V"])
        wt = wtmp[gi % 2]
        op("scalar", lambda q, sl=slot, wt=wt, mi=mi:
           q.activation(wt, wtile.ap()[:, sl * D:(sl + 1) * D], AF.Identity,
                        bias=0.0, scale=rwsb.ap()[:, mi:mi + 1]), sem="S")
        wtile_cons[slot] = ("S", C["S"])
        pp = gi % 2
        wait("vector", "S", C["S"])
        wait("vector", f"CW{pp}", C[f"CW{pp}"])
        op("vector", lambda q, wt=wt, pp=pp:
           q.tensor_scalar(wqi8.ap()[:, pp * D:(pp + 1) * D], wt, 1.0, -1.0,
                           ALU.min, ALU.max), sem="V")
        wait("gpsimd", "V", C["V"])
        wait("gpsimd", "TE", wq_te[pp])
        dma("gpsimd", wqb_v[:, pp * D:(pp + 1) * D],
            wqi8.ap()[:, pp * D:(pp + 1) * D], f"CW{pp}")
        wait("tensor", f"CW{pp}", C[f"CW{pp}"])
        wait("tensor", "V", C["V"])
        for kk in range(KD):
            bank = (gi % 2) * 2 + kk // 8
            sl = kk % 8
            op("tensor", lambda q, pp=pp, kk=kk, bank=bank, sl=sl:
               q.transpose(psB3[bank][:, sl, :],
                           wqb_v[:, pp * D + kk * P: pp * D + (kk + 1) * P],
                           identb.ap()[:]), sem=("TE" if kk % 8 == 7 else None))
        wq_te[pp] = C["TE"]
        te_settle_v()
        wait("vector", "TE", C["TE"])
        if dest_kind == "slab":
            sloti = dest
            if r == 0 and sloti in slab_guard:
                te_v, spa_n, spa_v = slab_guard[sloti]
                wait("vector", "TE", max(te_v, C["TE"]))
                wait("vector", spa_n, spa_v)
            for half in range(2):
                bank = (gi % 2) * 2 + half
                op("vector", lambda q, bank=bank, sloti=sloti, r=r, half=half:
                   q.tensor_copy(slab3[sloti][:, half * 8:(half + 1) * 8, r * P:(r + 1) * P],
                                 psB3[bank][:, :, :]), sem=("V" if half == 1 else None))
        else:
            j, qq = dest
            dsl = gi % 2
            nm = f"SPD{dsl}"
            wait("vector", nm, C[nm])
            d3 = dstg_v[:, dsl * D:(dsl + 1) * D].rearrange("p (e c) -> p e c", e=16)
            for half in range(2):
                bank = (gi % 2) * 2 + half
                op("vector", lambda q, bank=bank, d3=d3, half=half:
                   q.tensor_copy(d3[:, half * 8:(half + 1) * 8, :], psB3[bank][:, :, :]),
                   sem=("V" if half == 1 else None))
            wait("sync", "V", C["V"])
            for k2 in range(16):
                dma("sync", RdT[:, qq * 16 + k2, j * P:(j + 1) * P],
                    d3[:, k2, :], nm)

    psA_free = [("S", 0, "V", 0)] * 2
    psB_free_y = [("V", 0)] * 4

    def ffn(src_h, resid_h, dst_h, produce, tag):
        phase_L(src_h, do_quant_stats=True)
        mark()
        op("vector", lambda q: q.tensor_scalar(sgat[:], sc_[:], wsb.ap()[:, 0:1], None, ALU.mult))
        op("vector", lambda q: q.tensor_scalar(supc[:], sc_[:], wsb.ap()[:, 1:2], None, ALU.mult))
        op("vector", lambda q: q.memset(hmaxc[:], 0.0), sem="V")
        dr()

        for oc in range(OCA):
            for mi, mn in ((0, "g"), (1, "u")):
                slotd = mi * 2 + oc % 2
                if produce:
                    w_ap = wg_in if mn == "g" else wu_in
                    for r in range(4):
                        j = oc * 4 + r
                        pipe_piece(mn, w_ap[j * P:(j + 1) * P, :], "slab", slotd, r, mi)
                    pipe_v_slab[(mn, oc)] = C["V"]
                    Rt = RgT if mn == "g" else RuT
                    spa = f"SPA{oc % 2}"
                    wait("sync", "V", C["V"])
                    for k2 in range(KD):
                        dma("sync", Rt[:, k2, oc * 512:(oc + 1) * 512],
                            slab3[slotd][:, k2, :], spa)
                else:
                    Rt = RgT if mn == "g" else RuT
                    sl_sem = f"SL{slotd}"
                    if (tag, slotd) in slab_guard:
                        te_v = slab_guard[(tag, slotd)]
                        wait("sync", "TE", te_v)
                    for k2 in range(KD):
                        dma("sync", slab3[slotd][:, k2, :],
                            Rt[:, k2, oc * 512:(oc + 1) * 512], sl_sem)
            if produce:
                for r in range(4):
                    pi = oc * 4 + r
                    j = pi // 4
                    qq = pi % 4
                    pipe_piece("d", wd_in[j * P:(j + 1) * P, qq * D:(qq + 1) * D],
                               "dstage", (j, qq), 0, 2)

            slot_g = 0 + oc % 2
            slot_u = 2 + oc % 2
            if produce:
                wait("tensor", "V", max(pipe_v_slab[("g", oc)], pipe_v_slab[("u", oc)]))
            else:
                wait("tensor", f"SL{slot_g}", C[f"SL{slot_g}"])
                wait("tensor", f"SL{slot_u}", C[f"SL{slot_u}"])
            for t in range(TT):
                pr = t % 2
                sN, sV_, vN, vV_ = psA_free[pr]
                wait("tensor", vN, vV_)
                wait("tensor", sN, sV_)
                pg = psA[pr * 2].ap()
                pu = psA[pr * 2 + 1].ap()
                for k in range(KD):
                    st = (k == 0); sp = (k == KD - 1)
                    op("tensor", lambda q, k=k, t=t, pg=pg, sg=slot_g, st=st, sp=sp:
                       q.matmul(pg[:, 0:512], xqT3[:, k, t * P:(t + 1) * P],
                                slab3[sg][:, k, :], start=st, stop=sp))
                    op("tensor", lambda q, k=k, t=t, pu=pu, su=slot_u, st=st, sp=sp:
                       q.matmul(pu[:, 0:512], xqT3[:, k, t * P:(t + 1) * P],
                                slab3[su][:, k, :], start=st, stop=sp),
                       sem=("TE" if sp else None))
                mm_done = C["TE"]
                eh = (oc * TT + t) % 2
                wait("scalar", "TE", mm_done)
                wait("scalar", "V", psA_free[pr][3])
                op("scalar", lambda q, pg=pg, t=t, eh=eh:
                   q.activation(hgb.ap()[:, eh * 512:(eh + 1) * 512], pg[:, 0:512],
                                AF.Silu, bias=0.0, scale=sgat.ap()[:, t:t + 1]), sem="S")
                s_done = C["S"]
                wait("vector", "S", s_done)
                wait("vector", f"HS{eh}", C[f"HS{eh}"])
                op("vector", lambda q, pu=pu, t=t, eh=eh:
                   q.scalar_tensor_tensor(hob.ap()[:, eh * 512:(eh + 1) * 512], pu[:, 0:512],
                                          supc.ap()[:, t:t + 1],
                                          hgb.ap()[:, eh * 512:(eh + 1) * 512],
                                          ALU.mult, ALU.mult))
                dr()
                op("vector", lambda q, eh=eh:
                   q.tensor_reduce(tmp1c.ap()[:, 0:1], hob.ap()[:, eh * 512:(eh + 1) * 512],
                                   AX.X, ALU.max, apply_absolute_value=True))
                dr()
                op("vector", lambda q, t=t:
                   q.tensor_tensor(hmaxc.ap()[:, t:t + 1], hmaxc.ap()[:, t:t + 1],
                                   tmp1c.ap()[:, 0:1], ALU.max), sem="V")
                v_done = C["V"]
                psA_free[pr] = ("S", s_done, "V", v_done)
                wait("scalar", "V", v_done)
                dma("scalar", h_h[t * P:(t + 1) * P, oc * 512:(oc + 1) * 512],
                    hob.ap()[:, eh * 512:(eh + 1) * 512], f"HS{eh}")
            if not produce:
                slab_guard[(tag, slot_g)] = C["TE"]
                slab_guard[(tag, slot_u)] = C["TE"]
            else:
                slab_guard[slot_g] = (C["TE"], f"SPA{oc % 2}", C[f"SPA{oc % 2}"])
                slab_guard[slot_u] = (C["TE"], f"SPA{oc % 2}", C[f"SPA{oc % 2}"])

        mark()
        wait("vector", "V", C["V"])
        dr()
        op("vector", lambda q: q.tensor_scalar(s2c[:], hmaxc[:], INV127, EPS,
                                               ALU.mult, ALU.add))
        recip(lambda q: q.reciprocal(rs2c[:], s2c[:]), "rs2")
        op("vector", lambda q: q.tensor_scalar(scoc[:], s2c[:], wsb.ap()[:, 2:3], 0.5,
                                               ALU.mult, ALU.mult), sem="V")
        dr()
        V_H = C["V"]

        HS_ALL0, HS_ALL1 = C["HS0"], C["HS1"]

        for half in range(2):
            if half == 1:
                for e in ("scalar", "vector", "tensor", "gpsimd", "sync"):
                    wait(e, "TE", C["TE"])
                    wait(e, "V", C["V"])
                    wait(e, "S", C["S"])
                    wait(e, "YW0", C["YW0"])
                    wait(e, "YW1", C["YW1"])
            for tl in range(4):
                t = half * 4 + tl
                for hc in range(4):
                    ci = (t * 4 + hc) % 2
                    slot = ci
                    if wtile_cons[slot] is not None:
                        nm, vv = wtile_cons[slot]
                        wait("scalar", nm, vv)
                    if half == 0 and tl == 0 and hc == 0:
                        wait("scalar", "HS0", HS_ALL0)
                        wait("scalar", "HS1", HS_ALL1)
                    dma("scalar", wtile.ap()[:, slot * D:(slot + 1) * D],
                        h_h[t * P:(t + 1) * P, hc * D:(hc + 1) * D], f"LDW{slot}")
                    wait("scalar", f"LDW{slot}", C[f"LDW{slot}"])
                    wait("scalar", "V", V_H)
                    pp = ci
                    wait("scalar", f"CX{pp}", C[f"CX{pp}"])
                    op("scalar", lambda q, sl=slot, t=t, pp=pp:
                       q.activation(xqi8.ap()[:, pp * D:(pp + 1) * D],
                                    wtile.ap()[:, sl * D:(sl + 1) * D], AF.Identity,
                                    bias=0.0, scale=rs2c.ap()[:, t:t + 1]), sem="S")
                    wtile_cons[slot] = ("S", C["S"])
                    wait("gpsimd", "S", C["S"])
                    wait("gpsimd", "TE", xq_te[pp])
                    dma("gpsimd", xqb.ap()[:, pp * D:(pp + 1) * D],
                        xqi8.ap()[:, pp * D:(pp + 1) * D], f"CX{pp}")
                    wait("tensor", f"CX{pp}", C[f"CX{pp}"])
                    wait("tensor", "V", C["V"])
                    for kk in range(KD):
                        bank = ci * 2 + kk // 8
                        sl2 = kk % 8
                        op("tensor", lambda q, pp=pp, kk=kk, bank=bank, sl2=sl2:
                           q.transpose(psA3[bank][:, sl2, :],
                                       xqb.ap()[:, pp * D + kk * P: pp * D + (kk + 1) * P],
                                       identb.ap()[:]), sem=("TE" if kk % 8 == 7 else None))
                    xq_te[pp] = C["TE"]
                    te_settle_v()
                    wait("vector", "TE", C["TE"])
                    for hh in range(2):
                        bank = ci * 2 + hh
                        kbase = hc * 16 + hh * 8
                        op("vector", lambda q, bank=bank, kbase=kbase, tl=tl:
                           q.tensor_copy(hqT3[:, kbase:kbase + 8, tl * P:(tl + 1) * P],
                                         psA3[bank][:, :, :]), sem=("V" if hh == 1 else None))
            V_HQ = C["V"]
            mark()
            for oc in range(OCB):
                for kh in range(2):
                    nm = f"WD{kh}"
                    if ("wd", kh) in slab_guard:
                        wait("sync", "TE", slab_guard[("wd", kh)])
                    if oc == 0 and half == 0 and kh == 0:
                        wait("sync", "S", C["S"])
                        wait("sync", "V", C["V"])
                        wait("sync", "SPA0", C["SPA0"])
                        wait("sync", "SPA1", C["SPA1"])
                        if produce:
                            wait("sync", "SPD0", C["SPD0"])
                            wait("sync", "SPD1", C["SPD1"])
                    for k2 in range(32):
                        dma("sync", wdsl3[kh][:, k2, :],
                            RdT[:, kh * 32 + k2, oc * 256:(oc + 1) * 256], nm)
                wait("tensor", "V", V_HQ)
                for kh in range(2):
                    wait("tensor", f"WD{kh}", C[f"WD{kh}"])
                    for tl in range(4):
                        if kh == 0:
                            nmv, vv = psB_free_y[tl]
                            wait("tensor", nmv, vv)
                        for kk in range(32):
                            k = kh * 32 + kk
                            st = (k == 0); sp = (k == KH - 1)
                            op("tensor", lambda q, tl=tl, k=k, kk=kk, kh=kh, st=st, sp=sp:
                               q.matmul(psB[tl].ap()[:, 0:256],
                                        hqT3[:, k, tl * P:(tl + 1) * P],
                                        wdsl3[kh][:, kk, :], start=st, stop=sp),
                               sem=("TE" if sp else None))
                slab_guard[("wd", 0)] = C["TE"]
                slab_guard[("wd", 1)] = C["TE"]
                for tl in range(4):
                    t = half * 4 + tl
                    eh = (oc * 4 + tl) % 2
                    mm_done = C["TE"] - (3 - tl)
                    wait("scalar", "TE", mm_done)
                    wait("scalar", f"RS{eh}", C[f"RS{eh}"])
                    wait("scalar", "V", C["V"])
                    dma("scalar", rsb.ap()[:, eh * 512:eh * 512 + 256],
                        resid_h[t * P:(t + 1) * P, oc * 256:(oc + 1) * 256], f"RS{eh}")
                    wait("vector", f"RS{eh}", C[f"RS{eh}"])
                    wait("vector", "TE", mm_done)
                    wait("vector", f"YW{eh}", C[f"YW{eh}"])
                    op("vector", lambda q, tl=tl, t=t, eh=eh:
                       q.scalar_tensor_tensor(yob.ap()[:, eh * 512:eh * 512 + 256],
                                              psB[tl].ap()[:, 0:256],
                                              scoc.ap()[:, t:t + 1],
                                              rsb.ap()[:, eh * 512:eh * 512 + 256],
                                              ALU.mult, ALU.add), sem="V")
                    psB_free_y[tl] = ("V", C["V"])
                    wait("scalar", "V", C["V"])
                    dma("scalar", dst_h[t * P:(t + 1) * P, oc * 256:(oc + 1) * 256],
                        yob.ap()[:, eh * 512:eh * 512 + 256], f"YW{eh}")
                mark()
        mark("ffn_end")

    def attn():
        wait("scalar", "YW0", C["YW0"])
        wait("scalar", "YW1", C["YW1"])
        dma("sync", wcmv[0:64, :], wcm_in, "SM")
        dma("sync", dbc[0:1, :], dv_in, "SM")
        SM_W = C["SM"]
        phase_L(x1_h, do_quant_stats=False)
        V_ST = C["V"]
        mark()
        wait("tensor", "SM", SM_W)
        for ocq in range(4):
            bank = ocq % 2
            wait("tensor", "S", C["S"])
            wait("tensor", "V", C["V"])
            op("tensor", lambda q, ocq=ocq, bank=bank:
               q.matmul(psB[bank].ap()[:, 0:512], onesrow.ap()[:],
                        dbc[0:1, ocq * 512:(ocq + 1) * 512],
                        start=True, stop=True), sem="TE")
            te_settle_s()
            wait("scalar", "TE", C["TE"])
            op("scalar", lambda q, ocq=ocq, bank=bank:
               q.copy(dbc[:, ocq * 512:(ocq + 1) * 512], psB[bank].ap()[:, 0:512]), sem="S")
        mark()
        for t in range(TT):
            slot = t % 2
            xt_load(x1_h[t * P:(t + 1) * P, :], slot)
            wait("vector", f"XT{slot}", C[f"XT{slot}"])
            wait("vector", "V", V_ST)
            wait("vector", "S", C["S"])
            op("vector", lambda q, sl=slot, t=t:
               q.tensor_scalar(sqscr, xt.ap()[:, sl * D:(sl + 1) * D],
                               mcol.ap()[:, t:t + 1], None, ALU.subtract), sem="V")
            xt_cons[slot] = ("V", C["V"])
            wait("tensor", "V", C["V"])
            for kk in range(KD):
                bank = kk // 4
                sl2 = kk % 4
                op("tensor", lambda q, kk=kk, bank=bank, sl2=sl2:
                   q.transpose(psAf3[bank][:, sl2, :], sqscr[:, kk * P:(kk + 1) * P],
                               identf.ap()[:]), sem=("TE" if kk == KD - 1 else None))
            te_settle_v()
            wait("vector", "TE", C["TE"])
            for bank in range(4):
                op("vector", lambda q, bank=bank, t=t:
                   q.tensor_copy(xcT3[:, bank * 4:(bank + 1) * 4, t * P:(t + 1) * P],
                                 psAf3[bank][:, :, :]), sem=("V" if bank == 3 else None))
        mark()
        wait("tensor", "SM", SM_W)
        for tc in range(2):
            wait("tensor", "V", C["V"])
            wait("tensor", "S", C["S"])
            for k in range(KD):
                st = (k == 0); sp = (k == KD - 1)
                op("tensor", lambda q, k=k, tc=tc, st=st, sp=sp:
                   q.matmul(psA[2].ap()[0:64, 0:512], wleft.ap()[:, k * 64:(k + 1) * 64],
                            xcT3[:, k, tc * 512:(tc + 1) * 512], start=st, stop=sp),
                   sem=("TE" if sp else None))
            te_settle_s()
            wait("scalar", "TE", C["TE"])
            op("scalar", lambda q, tc=tc:
               q.copy(t1Tv[0:64, tc * 512:(tc + 1) * 512], psA[2].ap()[0:64, 0:512]), sem="S")
        S_T1T = C["S"]
        mark()
        y_guard = ("V", C["V"])
        for t in range(TT):
            for ocq in range(4):
                wait("tensor", "S", S_T1T)
                nmv, vv = y_guard
                wait("tensor", nmv, vv)
                op("tensor", lambda q, t=t, ocq=ocq:
                   q.matmul(psA[3].ap()[:, 0:512], t1Tv[0:64, t * P:(t + 1) * P],
                            wcmv[0:64, ocq * 512:(ocq + 1) * 512],
                            start=True, stop=True), sem="TE")
                mm_done = C["TE"]
                eh = (t * 4 + ocq) % 2
                wait("scalar", f"RS{eh}", C[f"RS{eh}"])
                wait("scalar", "V", C["V"])
                dma("scalar", rsb.ap()[:, eh * 512:(eh + 1) * 512],
                    x1_h[t * P:(t + 1) * P, ocq * 512:(ocq + 1) * 512], f"RS{eh}")
                wait("vector", f"RS{eh}", C[f"RS{eh}"])
                wait("vector", "TE", mm_done)
                wait("vector", f"YW{eh}", C[f"YW{eh}"])
                op("vector", lambda q, t=t, eh=eh:
                   q.tensor_scalar(hgb.ap()[:, eh * 512:(eh + 1) * 512],
                                   rsb.ap()[:, eh * 512:(eh + 1) * 512],
                                   mcol.ap()[:, t:t + 1], None, ALU.subtract))
                dr()
                op("vector", lambda q, eh=eh, ocq=ocq:
                   q.tensor_tensor(hgb.ap()[:, eh * 512:(eh + 1) * 512],
                                   hgb.ap()[:, eh * 512:(eh + 1) * 512],
                                   dbc[:, ocq * 512:(ocq + 1) * 512], ALU.mult))
                dr()
                op("vector", lambda q, eh=eh:
                   q.tensor_tensor(hob.ap()[:, eh * 512:(eh + 1) * 512],
                                   hgb.ap()[:, eh * 512:(eh + 1) * 512],
                                   psA[3].ap()[:, 0:512], ALU.add))
                dr()
                op("vector", lambda q, t=t, eh=eh:
                   q.scalar_tensor_tensor(yob.ap()[:, eh * 512:(eh + 1) * 512],
                                          hob.ap()[:, eh * 512:(eh + 1) * 512],
                                          rstdc.ap()[:, t:t + 1],
                                          rsb.ap()[:, eh * 512:(eh + 1) * 512],
                                          ALU.mult, ALU.add), sem="V")
                y_guard = ("V", C["V"])
                wait("scalar", "V", C["V"])
                dma("scalar", x2_h[t * P:(t + 1) * P, ocq * 512:(ocq + 1) * 512],
                    yob.ap()[:, eh * 512:(eh + 1) * 512], f"YW{eh}")

    ffn(u_in, u_in, x1_h, produce=True, tag=1)
    attn()
    mark("attn_end")
    wait("scalar", "YW0", C["YW0"])
    wait("scalar", "YW1", C["YW1"])
    wait("sync", "SPA0", C["SPA0"])
    wait("sync", "SPA1", C["SPA1"])
    wait("sync", "SPD0", C["SPD0"])
    wait("sync", "SPD1", C["SPD1"])
    wait("sync", "TE", C["TE"])
    wait("sync", "V", C["V"])
    ffn(x2_h, x2_h, out_d, produce=False, tag=2)

    wait("gpsimd", "YW0", C["YW0"])
    wait("gpsimd", "YW1", C["YW1"])
    mark()
    build.last_marks = marks

    if isinstance(phases, str):
        if phases == "ffn1_end":
            phases = [i for i, m in enumerate(marks) if m[2] == "ffn_end"][0] + 1
        elif phases == "ffn2_end":
            phases = [i for i, m in enumerate(marks) if m[2] == "ffn_end"][1] + 1
        elif phases == "attn_end":
            phases = [i for i, m in enumerate(marks) if m[2] == "attn_end"][0] + 1
    if phases is not None and phases < len(marks):
        cut, cutC, _lbl = marks[phases]
        for k in prog:
            prog[k] = prog[k][:cut[k]]
    else:
        cutC = dict(C)
    if dbg is not None:
        for nm, vv in cutC.items():
            p("gpsimd", lambda q, s_=S[nm], v_=vv: q.wait_ge(s_, v_))
        ndma = 0
        if dbg == "arena0":
            for c2 in range(8):
                p("gpsimd", lambda q, c2=c2: q.dma_start(
                    out=out_d[c2 * 128:(c2 + 1) * 128, :],
                    in_=AFp[:, c2 * 2048:(c2 + 1) * 2048]).then_inc(S["SM"], 16))
                ndma += 1
        elif dbg == "arena1":
            for c2 in range(8):
                p("gpsimd", lambda q, c2=c2: q.dma_start(
                    out=out_d[c2 * 128:(c2 + 1) * 128, :],
                    in_=AFp[:, 16384 + c2 * 2048:16384 + (c2 + 1) * 2048]).then_inc(S["SM"], 16))
                ndma += 1
        elif dbg in ("x1", "x2", "hq0"):
            srcT = {"x1": x1_h, "x2": x2_h}.get(dbg)
            if dbg == "hq0":
                srcT = h_h[:, 0:2048]
            for c2 in range(8):
                p("gpsimd", lambda q, c2=c2, srcT=srcT: q.dma_start(
                    out=out_d[c2 * 128:(c2 + 1) * 128, :],
                    in_=srcT[c2 * 128:(c2 + 1) * 128, :]).then_inc(S["SM"], 16))
                ndma += 1
        elif dbg == "wgT":
            bt = wgT_h.bitcast(F32)
            for c2 in range(8):
                p("gpsimd", lambda q, c2=c2, bt=bt: q.dma_start(
                    out=out_d[c2 * 128:(c2 + 1) * 128, :],
                    in_=bt[c2 * 128:(c2 + 1) * 128, 0:2048]).then_inc(S["SM"], 16))
                ndma += 1
        elif dbg == "stats":
            dumps = [wsb, rwsb, sc_, rsc, rstdc, qqc, nmqc, sgat, supc, hmaxc,
                     s2c, rs2c, scoc, mcol, ssqc, vc, sdc, wsraw, amaxc, sumc]
            for i2, tens in enumerate(dumps):
                w2 = tens.ap().shape[1]
                p("gpsimd", lambda q, i2=i2, tens=tens, w2=w2: q.dma_start(
                    out=out_d[0:128, i2 * 8:i2 * 8 + w2],
                    in_=tens.ap()[:, 0:w2]).then_inc(S["SM"], 16))
                ndma += 1
        elif dbg == "wdT":
            bt = wdT_h.bitcast(F32)
            for c2 in range(8):
                p("gpsimd", lambda q, c2=c2, bt=bt: q.dma_start(
                    out=out_d[c2 * 128:(c2 + 1) * 128, 0:1024],
                    in_=bt[c2 * 128:(c2 + 1) * 128, 0:1024]).then_inc(S["SM"], 16))
                ndma += 1
        tot = cutC["SM"] + 16 * ndma
        p("gpsimd", lambda q, tot=tot: q.wait_ge(S["SM"], tot))

    with es, nc.Block() as blk:
        @blk.sync
        def _(q):
            for f in prog["sync"]:
                f(q)

        @blk.scalar
        def _(q):
            for f in prog["scalar"]:
                f(q)

        @blk.vector
        def _(q):
            for f in prog["vector"]:
                f(q)

        @blk.tensor
        def _(q):
            for f in prog["tensor"]:
                f(q)

        @blk.gpsimd
        def _(q):
            for f in prog["gpsimd"]:
                f(q)
    return nc


_NC_CACHE = []


def kernel(L0_attn_u, L0_attn_A_factors, L0_attn_W_b_rank, L0_attn_W_b_state,
           L0_attn_W_c_rank, L0_attn_W_c_model, L0_attn_D_vec,
           L0_ffn_w_gate, L0_ffn_w_up, L0_ffn_w_down):
    u = np.ascontiguousarray(np.asarray(L0_attn_u, dtype=np.float32).reshape(8192, 2048))
    wg = np.ascontiguousarray(np.asarray(L0_ffn_w_gate, dtype=np.float32))
    wu = np.ascontiguousarray(np.asarray(L0_ffn_w_up, dtype=np.float32))
    wd = np.ascontiguousarray(np.asarray(L0_ffn_w_down, dtype=np.float32))
    a_fl = np.ascontiguousarray(np.asarray(L0_attn_A_factors, dtype=np.float32).reshape(1, 32))
    wbr = np.ascontiguousarray(np.asarray(L0_attn_W_b_rank, dtype=np.float32))
    wbs = np.ascontiguousarray(np.asarray(L0_attn_W_b_state, dtype=np.float32))
    wcr = np.ascontiguousarray(np.asarray(L0_attn_W_c_rank, dtype=np.float32))
    wcm = np.ascontiguousarray(np.asarray(L0_attn_W_c_model, dtype=np.float32))
    dv = np.ascontiguousarray(np.asarray(L0_attn_D_vec, dtype=np.float32).reshape(1, 2048))
    idb = np.eye(P, dtype=np.float32).astype(ml_dtypes.bfloat16)
    idf = np.eye(P, dtype=np.float32)
    onc = np.ones((P, 1), dtype=np.float32)
    onr = np.ones((1, P), dtype=np.float32)

    if not _NC_CACHE:
        _NC_CACHE.append(build())
    nc = _NC_CACHE[0]

    in_maps = []
    for c in range(8):
        in_maps.append(dict(u=np.ascontiguousarray(u[c * TOK:(c + 1) * TOK]),
                            wg=wg, wu=wu, wd=wd, a_fl=a_fl, wbr=wbr, wbs=wbs,
                            wcr=wcr, wcm=wcm, dv=dv, idb=idb, idf=idf,
                            onc=onc, onr=onr))
    res = run_bass_kernel_spmd(nc, in_maps, core_ids=list(range(8)))
    out = np.concatenate([res.results[c]["out"] for c in range(8)], axis=0)
    return out.reshape(4, 2048, 2048)


if __name__ == "__main__":
    nc = build()
    print("build OK")


# version 04
# speedup vs baseline: 1.0750x; 1.0750x over previous
import numpy as np
import ml_dtypes
import contextlib
import concourse.bass as bass
import concourse.mybir as mybir
from concourse.bass_utils import run_bass_kernel_spmd

F32 = mybir.dt.float32
BF16 = mybir.dt.bfloat16
I8 = mybir.dt.int8
AF = mybir.ActivationFunctionType
ALU = mybir.AluOpType
AX = mybir.AxisListType

P = 128
TOK = 1024
TT = 8
D = 2048
KD = 16
H = 8192
KH = 64
OCA = 16
OCB = 8
INV127 = 1.0 / 127.0
INV2048 = 1.0 / 2048.0
INV2_24 = 1.0 / float(1 << 24)
EPS = 1e-5
BARRIER = False


def build(phases=None, dbg=None):
    nc = bass.Bass("TRN2", target_bir_lowering=False)

    u_in = nc.dram_tensor("u", [TOK, D], F32, kind="ExternalInput").ap()
    wg_in = nc.dram_tensor("wg", [H, D], F32, kind="ExternalInput").ap()
    wu_in = nc.dram_tensor("wu", [H, D], F32, kind="ExternalInput").ap()
    wd_in = nc.dram_tensor("wd", [D, H], F32, kind="ExternalInput").ap()
    a_fl = nc.dram_tensor("a_fl", [1, 32], F32, kind="ExternalInput").ap()
    wbr_in = nc.dram_tensor("wbr", [D, 64], F32, kind="ExternalInput").ap()
    wbs_in = nc.dram_tensor("wbs", [64, 256], F32, kind="ExternalInput").ap()
    wcr_in = nc.dram_tensor("wcr", [256, 64], F32, kind="ExternalInput").ap()
    wcm_in = nc.dram_tensor("wcm", [64, D], F32, kind="ExternalInput").ap()
    dv_in = nc.dram_tensor("dv", [1, D], F32, kind="ExternalInput").ap()
    idb_in = nc.dram_tensor("idb", [P, P], BF16, kind="ExternalInput").ap()
    idf_in = nc.dram_tensor("idf", [P, P], F32, kind="ExternalInput").ap()
    onc_in = nc.dram_tensor("onc", [P, 1], F32, kind="ExternalInput").ap()
    onr_in = nc.dram_tensor("onr", [1, P], F32, kind="ExternalInput").ap()

    out_d = nc.dram_tensor("out", [TOK, D], F32, kind="ExternalOutput").ap()

    wgT_h = nc.dram_tensor("wgT_h", [D, H], BF16, kind="Internal").ap()
    wuT_h = nc.dram_tensor("wuT_h", [D, H], BF16, kind="Internal").ap()
    wdT_h = nc.dram_tensor("wdT_h", [H, D], BF16, kind="Internal").ap()
    h_h = nc.dram_tensor("h_h", [TOK, H], F32, kind="Internal").ap()
    x1_h = nc.dram_tensor("x1_h", [TOK, D], F32, kind="Internal").ap()
    x2_h = nc.dram_tensor("x2_h", [TOK, D], F32, kind="Internal").ap()

    RgT = wgT_h.rearrange("(k p) o -> p k o", p=P)
    RuT = wuT_h.rearrange("(k p) o -> p k o", p=P)
    RdT = wdT_h.rearrange("(k p) o -> p k o", p=P)

    es = contextlib.ExitStack()
    sb = lambda name, shp, dt: es.enter_context(nc.sbuf_tensor(name, shp, dt))

    arena = sb("arena", [P, 32768], F32)
    AFp = arena.ap()
    ABp = AFp.bitcast(BF16)

    def fa(b0, b1):
        return AFp[:, b0 // 4: b1 // 4]

    def fb(b0, b1):
        return ABp[:, b0 // 2: b1 // 2]

    xqT3 = fb(0, 32768).rearrange("p (k t) -> p k t", k=KD)
    slab3 = [fb(32768 + i * 16384, 32768 + (i + 1) * 16384)
             .rearrange("p (k c) -> p k c", k=KD) for i in range(4)]
    hqT3 = fb(0, 65536).rearrange("p (k c) -> p k c", k=KH)
    wdsl3 = [fb(65536 + i * 16384, 65536 + (i + 1) * 16384)
             .rearrange("p (k c) -> p k c", k=32) for i in range(2)]
    wtmp = [fa(98304 + i * 8192, 98304 + (i + 1) * 8192) for i in range(2)]
    sqscr = fa(114688, 122880)
    wqb_v = fb(114688, 122880)
    dstg_v = fb(122880, 131072)
    xcT3 = fa(0, 65536).rearrange("p (k t) -> p k t", k=KD)
    dbc = fa(65536, 73728)
    wcmv = fa(73728, 81920)
    t1Tv = fa(81920, 86016)

    xt = sb("xt", [P, 2 * D], F32)
    wtile = sb("wtile", [P, 2 * D], F32)
    xqi8 = sb("xqi8", [P, 2 * D], I8)
    xqb = sb("xqb", [P, 2 * D], BF16)
    wqi8 = sb("wqi8", [P, 2 * D], I8)
    hgb = sb("hgb", [P, 2 * 512], F32)
    hob = sb("hob", [P, 2 * 512], F32)
    yob = sb("yob", [P, 2 * 512], F32)
    rsb = sb("rsb", [P, 2 * 512], F32)
    wleft = sb("wleft", [P, KD * 64], F32)
    redc = sb("redc", [P, 3 * 64], F32)
    identb = sb("identb", [P, P], BF16)
    identf = sb("identf", [P, P], F32)
    onescol = sb("onescol", [P, 1], F32)
    onesrow = sb("onesrow", [1, P], F32)
    a_bc = sb("a_bc", [64, 32], F32)
    wbs_a = sb("wbs_a", [64, 256], F32)
    wbs_b = sb("wbs_b", [64, 256], F32)
    btmp = sb("btmp", [64, 128], F32)
    wbsmT = sb("wbsmT", [P, P], F32)
    wmid = sb("wmid", [64, 64], F32)
    wbrT1 = sb("wbrT1", [64, P], F32)
    wbr_sb = sb("wbr_sb", [P, KD * 64], F32)
    wcr_sb = sb("wcr_sb", [P, 2 * 64], F32)
    totsb = sb("totsb", [1, 4], F32)

    def stat(name):
        return sb(name, [P, 8], F32)

    sumc = stat("sumc"); xmaxc = stat("xmaxc"); xminc = stat("xminc")
    ssqc = stat("ssqc"); mcol = stat("mcol"); nmcol = stat("nmcol")
    vc = stat("vc"); sdc = stat("sdc"); rstdc = stat("rstdc")
    amax1 = stat("amax1"); amax2 = stat("amax2"); amaxc = stat("amaxc")
    sc_ = stat("sc_"); rsc = stat("rsc"); qqc = stat("qqc"); nmqc = stat("nmqc")
    sgat = stat("sgat"); supc = stat("supc")
    hmaxc = stat("hmaxc"); s2c = stat("s2c"); rs2c = stat("rs2c"); scoc = stat("scoc")
    tmp1c = stat("tmp1c")
    wsraw = sb("wsraw", [P, 3], F32)
    wsb = sb("wsb", [P, 3], F32)
    rwsb = sb("rwsb", [P, 3], F32)

    psA = [es.enter_context(nc.psum_tensor(f"psA{i}", [P, 512], F32)) for i in range(4)]
    psB = [es.enter_context(nc.psum_tensor(f"psB{i}", [P, 512], F32)) for i in range(4)]
    psAb = [t.ap().bitcast(BF16) for t in psA]
    psBb = [t.ap().bitcast(BF16) for t in psB]
    psA3 = [t.ap().bitcast(BF16).rearrange("p (e c) -> p e c", e=8) for t in psA]
    psB3 = [t.ap().bitcast(BF16).rearrange("p (e c) -> p e c", e=8) for t in psB]
    psAf3 = [t.ap().rearrange("p (e c) -> p e c", e=4) for t in psA]

    sem_names = ["SM", "LDW0", "LDW1", "XT0", "XT1", "CW0", "CW1", "CX0", "CX1",
                 "SL0", "SL1", "SL2", "SL3", "WD0", "WD1", "HS0", "HS1",
                 "RS0", "RS1", "YW0", "YW1", "SPA0", "SPA1", "SPD0", "SPD1",
                 "V", "S", "TE"]
    S = {n: es.enter_context(nc.semaphore(n)) for n in sem_names}
    C = {n: 0 for n in sem_names}

    prog = {k: [] for k in ("sync", "scalar", "vector", "tensor", "gpsimd")}
    marks = []

    def mark(label=None):
        marks.append(({k: len(v) for k, v in prog.items()}, dict(C), label))

    def p(eng, fn):
        prog[eng].append(fn)

    def wait(eng, name, val=None):
        v = C[name] if val is None else val
        p(eng, lambda q, s=S[name], vv=v: q.wait_ge(s, vv))

    def dma(eng, out, in_, sem):
        p(eng, lambda q, o=out, i=in_, s=S[sem]: q.dma_start(out=o, in_=i).then_inc(s, 16))
        C[sem] += 16

    def op(eng, fn, sem=None):
        if sem is None:
            p(eng, fn)
        else:
            p(eng, lambda q, f=fn, s=S[sem]: f(q).then_inc(s, 1))
            C[sem] += 1

    def dr():
        p("vector", lambda q: q.drain())

    def vs_fence():
        op("vector", lambda q: q.tensor_copy(totsb.ap()[0:1, 3:4], totsb.ap()[0:1, 2:3]),
           sem="V")
        wait("scalar", "V", C["V"])
        op("scalar", lambda q: q.copy(totsb.ap()[0:1, 2:3], totsb.ap()[0:1, 3:4]), sem="S")
        wait("vector", "S", C["S"])

    def recip(dst_fn, src_desc):
        vs_fence()
        op("vector", dst_fn, sem="V")
        vs_fence()

    def te_settle_v():
        wait("scalar", "TE", C["TE"])
        op("scalar", lambda q: q.copy(totsb.ap()[0:1, 1:2], totsb.ap()[0:1, 0:1]), sem="S")
        wait("vector", "S", C["S"])

    def te_settle_s():
        wait("vector", "TE", C["TE"])
        op("vector", lambda q: q.tensor_copy(totsb.ap()[0:1, 1:2], totsb.ap()[0:1, 0:1]),
           sem="V")
        wait("scalar", "V", C["V"])

    dma("sync", identb[:], idb_in, "SM")
    dma("sync", identf[:], idf_in, "SM")
    dma("sync", onescol[:], onc_in, "SM")
    dma("sync", onesrow[:], onr_in, "SM")
    dma("sync", a_bc[:], a_fl.partition_broadcast(64), "SM")
    dma("sync", wbs_a[:], wbs_in, "SM")
    dma("sync", wcr_sb[:, 0:64], wcr_in[0:128, :], "SM")
    dma("sync", wcr_sb[:, 64:128], wcr_in[128:256, :], "SM")
    for k2 in range(KD):
        dma("sync", wbr_sb.ap()[:, k2 * 64:(k2 + 1) * 64],
            wbr_in[k2 * P:(k2 + 1) * P, :], "SM")
    SM_ALL = C["SM"]

    wait("vector", "SM", SM_ALL)
    for lvl in range(8):
        src = (wbs_a if lvl % 2 == 0 else wbs_b).ap()
        dst = (wbs_b if lvl % 2 == 0 else wbs_a).ap()
        pre = 1 << lvl
        post = 128 >> lvl
        sv = src.rearrange("p (pre two post) -> p pre two post", pre=pre, two=2)
        dvv = dst.rearrange("p (pre two post) -> p pre two post", pre=pre, two=2)
        a00 = a_bc.ap()[:, 4 * lvl + 0: 4 * lvl + 1]
        a01 = a_bc.ap()[:, 4 * lvl + 1: 4 * lvl + 2]
        a10 = a_bc.ap()[:, 4 * lvl + 2: 4 * lvl + 3]
        a11 = a_bc.ap()[:, 4 * lvl + 3: 4 * lvl + 4]
        tb = btmp.ap()[:, 0:128].rearrange("p (pre post) -> p pre post", pre=pre)
        last = (lvl == 7)
        dr()
        op("vector", lambda q, o=tb, i=sv, a=a00: q.tensor_scalar(o, i[:, :, 0, :], a, None, ALU.mult))
        dr()
        op("vector", lambda q, o=dvv, i=sv, a=a10, t=tb:
           q.scalar_tensor_tensor(o[:, :, 0, :], i[:, :, 1, :], a, t, ALU.mult, ALU.add))
        dr()
        op("vector", lambda q, o=tb, i=sv, a=a01: q.tensor_scalar(o, i[:, :, 0, :], a, None, ALU.mult))
        dr()
        op("vector", lambda q, o=dvv, i=sv, a=a11, t=tb:
           q.scalar_tensor_tensor(o[:, :, 1, :], i[:, :, 1, :], a, t, ALU.mult, ALU.add),
           sem="V" if last else None)
    V_BFLY = C["V"]
    wbs_fin = wbs_b
    wbs_fin = wbs_a if (8 % 2 == 0) else wbs_b
    wbs_fin = wbs_a

    wait("tensor", "V", V_BFLY)
    op("tensor", lambda q: q.transpose(psB[0].ap()[0:128, 0:64], wbs_fin.ap()[:, 0:128],
                                       identf.ap()[0:64, 0:64]))
    op("tensor", lambda q: q.transpose(psB[0].ap()[0:128, 64:128], wbs_fin.ap()[:, 128:256],
                                       identf.ap()[0:64, 0:64]), sem="TE")
    TE_T1 = C["TE"]
    te_settle_s()
    wait("scalar", "TE", TE_T1)
    op("scalar", lambda q: q.copy(wbsmT[:], psB[0].ap()[0:128, 0:128]), sem="S")
    S_T1 = C["S"]
    wait("tensor", "S", S_T1)
    op("tensor", lambda q: q.matmul(psB[1].ap()[0:64, 0:64], wbsmT.ap()[:, 0:64],
                                    wcr_sb.ap()[:, 0:64], start=True, stop=False))
    op("tensor", lambda q: q.matmul(psB[1].ap()[0:64, 0:64], wbsmT.ap()[:, 64:128],
                                    wcr_sb.ap()[:, 64:128], start=False, stop=True), sem="TE")
    TE_T2 = C["TE"]
    te_settle_s()
    wait("scalar", "TE", TE_T2)
    op("scalar", lambda q: q.copy(wmid[:], psB[1].ap()[0:64, 0:64]), sem="S")
    S_T2 = C["S"]
    wait("tensor", "S", S_T2)
    for k in range(KD):
        wait("tensor", "S", C["S"])
        op("tensor", lambda q, k=k: q.transpose(psB[0].ap()[0:64, 0:128],
                                                wbr_sb.ap()[:, k * 64:(k + 1) * 64],
                                                identf.ap()[0:128, 0:128]), sem="TE")
        te_settle_s()
        wait("scalar", "TE", C["TE"])
        op("scalar", lambda q: q.copy(wbrT1[:], psB[0].ap()[0:64, 0:128]), sem="S")
        wait("tensor", "S", C["S"])
        op("tensor", lambda q: q.matmul(psB[1].ap()[0:128, 0:64], wbrT1.ap()[:],
                                        wmid.ap()[:], start=True, stop=True), sem="TE")
        te_settle_s()
        wait("scalar", "TE", C["TE"])
        op("scalar", lambda q, k=k: q.copy(wleft.ap()[:, k * 64:(k + 1) * 64],
                                           psB[1].ap()[0:128, 0:64]), sem="S")
    S_SETUP = C["S"]
    mark()

    wtile_use = [0, 0]
    wtile_cons = [None, None]

    def wtile_load(src_ap, slot, q="sync"):
        if wtile_cons[slot] is not None:
            nm, vv = wtile_cons[slot]
            wait(q, nm, vv)
        dma(q, wtile.ap()[:, slot * D:(slot + 1) * D], src_ap, f"LDW{slot}")

    mats = [("g", wg_in, redc.ap()[:, 0:64]), ("u", wu_in, redc.ap()[:, 64:128]),
            ("d", wd_in, redc.ap()[:, 128:192])]
    for mi, (mn, w_ap, rcol) in enumerate(mats):
        for j in range(64):
            slot = (mi * 64 + j) % 2
            if mn == "d":
                src = w_ap[(j // 4) * P:(j // 4 + 1) * P, (j % 4) * D:(j % 4 + 1) * D]
            else:
                src = w_ap[j * P:(j + 1) * P, :]
            wtile_load(src, slot, q=("sync" if slot == 0 else "gpsimd"))
            wait("vector", f"LDW{slot}", C[f"LDW{slot}"])
            op("vector", lambda q, o=rcol, j=j, sl=slot:
               q.tensor_reduce(o[:, j:j + 1], wtile.ap()[:, sl * D:(sl + 1) * D],
                               AX.X, ALU.add, apply_absolute_value=True), sem="V")
            wtile_cons[slot] = ("V", C["V"])
    for mi in range(3):
        rcol = redc.ap()[:, mi * 64:(mi + 1) * 64]
        dr()
        op("vector", lambda q, r=rcol, mi=mi:
           q.tensor_reduce(wsraw.ap()[:, mi:mi + 1], r, AX.X, ALU.add), sem="V")
        wait("tensor", "V", C["V"])
        op("tensor", lambda q, mi=mi: q.matmul(psB[0].ap()[0:1, mi:mi + 1],
                                               wsraw.ap()[:, mi:mi + 1],
                                               onescol.ap()[:], start=True, stop=True), sem="TE")
        te_settle_s()
        wait("scalar", "TE", C["TE"])
        op("scalar", lambda q, mi=mi: q.copy(totsb.ap()[0:1, mi:mi + 1],
                                             psB[0].ap()[0:1, mi:mi + 1]), sem="S")
        wait("tensor", "S", C["S"])
        op("tensor", lambda q, mi=mi: q.matmul(psB[1].ap()[0:128, mi:mi + 1],
                                               onesrow.ap()[:],
                                               totsb.ap()[0:1, mi:mi + 1],
                                               start=True, stop=True), sem="TE")
        te_settle_s()
        wait("scalar", "TE", C["TE"])
        op("scalar", lambda q, mi=mi: q.copy(wsraw.ap()[:, mi:mi + 1],
                                             psB[1].ap()[0:128, mi:mi + 1]), sem="S")
        wait("vector", "S", C["S"])
        op("vector", lambda q, mi=mi: q.tensor_scalar(wsb.ap()[:, mi:mi + 1],
                                                      wsraw.ap()[:, mi:mi + 1],
                                                      INV2_24, EPS, ALU.mult, ALU.add))
        recip(lambda q, mi=mi: q.reciprocal(rwsb.ap()[:, mi:mi + 1],
                                            wsb.ap()[:, mi:mi + 1]), "rws")
    V_WS = C["V"]
    mark()

    xt_use = [0, 0]
    xt_cons = [None, None]

    def xt_load(src_ap, slot, extra_wait=None):
        if xt_cons[slot] is not None:
            nm, vv = xt_cons[slot]
            wait("scalar", nm, vv)
        if extra_wait is not None:
            wait("scalar", extra_wait[0], extra_wait[1])
        dma("scalar", xt.ap()[:, slot * D:(slot + 1) * D], src_ap, f"XT{slot}")

    xq_te = [0, 0]

    def phase_L(src_h, do_quant_stats=True):
        for t in range(TT):
            slot = t % 2
            xt_load(src_h[t * P:(t + 1) * P, :], slot)
            wait("vector", f"XT{slot}", C[f"XT{slot}"])
            xv = xt.ap()[:, slot * D:(slot + 1) * D]
            op("vector", lambda q, xv=xv, t=t:
               q.tensor_reduce(sumc.ap()[:, t:t + 1], xv, AX.X, ALU.add))
            if do_quant_stats:
                op("vector", lambda q, xv=xv, t=t:
                   q.tensor_reduce(xmaxc.ap()[:, t:t + 1], xv, AX.X, ALU.max))
                op("vector", lambda q, xv=xv, t=t:
                   q.tensor_reduce(xminc.ap()[:, t:t + 1], xv, AX.X, ALU.min))
            dr()
            op("vector", lambda q, t=t:
               q.tensor_scalar(mcol.ap()[:, t:t + 1], sumc.ap()[:, t:t + 1],
                               INV2048, None, ALU.mult))
            dr()
            op("vector", lambda q, t=t:
               q.tensor_scalar(nmcol.ap()[:, t:t + 1], mcol.ap()[:, t:t + 1],
                               -1.0, None, ALU.mult), sem="V")
            wait("scalar", "V", C["V"])
            op("scalar", lambda q, xv=xv, t=t:
               q.activation(sqscr, xv, AF.Square, bias=nmcol.ap()[:, t:t + 1],
                            scale=1.0, accum_out=ssqc.ap()[:, t:t + 1]), sem="S")
            xt_cons[slot] = ("S", C["S"])
        wait("vector", "S", C["S"])
        op("vector", lambda q: q.tensor_scalar(vc[:], ssqc[:], INV2048, EPS,
                                               ALU.mult, ALU.add), sem="V")
        wait("scalar", "V", C["V"])
        op("scalar", lambda q: q.activation(sdc[:], vc[:], AF.Sqrt), sem="S")
        wait("vector", "S", C["S"])
        recip(lambda q: q.reciprocal(rstdc[:], sdc[:]), "rstd")
        if do_quant_stats:
            dr()
            op("vector", lambda q: q.tensor_tensor(amax1[:], xmaxc[:], mcol[:], ALU.subtract))
            op("vector", lambda q: q.tensor_tensor(amax2[:], mcol[:], xminc[:], ALU.subtract))
            dr()
            op("vector", lambda q: q.tensor_tensor(amaxc[:], amax1[:], amax2[:], ALU.max))
            dr()
            op("vector", lambda q: q.tensor_tensor(tmp1c[:], amaxc[:], rstdc[:], ALU.mult))
            dr()
            op("vector", lambda q: q.tensor_scalar(sc_[:], tmp1c[:], INV127, EPS,
                                                   ALU.mult, ALU.add))
            recip(lambda q: q.reciprocal(rsc[:], sc_[:]), "rsc")
            op("vector", lambda q: q.tensor_tensor(qqc[:], rstdc[:], rsc[:], ALU.mult))
            dr()
            op("vector", lambda q: q.tensor_tensor(nmqc[:], nmcol[:], qqc[:], ALU.mult),
               sem="V")
        else:
            op("vector", lambda q: q.tensor_copy(tmp1c.ap()[:, 0:1], rstdc.ap()[:, 0:1]),
               sem="V")
        V_STATS = C["V"]

        if not do_quant_stats:
            return

        for t in range(TT):
            slot = t % 2
            xt_load(src_h[t * P:(t + 1) * P, :], slot)
            wait("scalar", f"XT{slot}", C[f"XT{slot}"])
            wait("scalar", "V", V_STATS)
            pp = t % 2
            wait("scalar", f"CX{pp}", C[f"CX{pp}"])
            xv = xt.ap()[:, slot * D:(slot + 1) * D]
            op("scalar", lambda q, xv=xv, t=t, pp=pp:
               q.activation(xqi8.ap()[:, pp * D:(pp + 1) * D], xv, AF.Identity,
                            bias=nmqc.ap()[:, t:t + 1], scale=qqc.ap()[:, t:t + 1]), sem="S")
            xt_cons[slot] = ("S", C["S"])
            wait("gpsimd", "S", C["S"])
            wait("gpsimd", "TE", xq_te[pp])
            dma("gpsimd", xqb.ap()[:, pp * D:(pp + 1) * D],
                xqi8.ap()[:, pp * D:(pp + 1) * D], f"CX{pp}")
            wait("tensor", f"CX{pp}", C[f"CX{pp}"])
            wait("tensor", "V", C["V"])
            wait("tensor", "S", C["S"])
            for kk in range(KD):
                bank = (t % 2) * 2 + kk // 8
                sl = kk % 8
                op("tensor", lambda q, pp=pp, kk=kk, bank=bank, sl=sl:
                   q.transpose(psB3[bank][:, sl, :],
                               xqb.ap()[:, pp * D + kk * P: pp * D + (kk + 1) * P],
                               identb.ap()[:]), sem=("TE" if kk % 8 == 7 else None))
            xq_te[pp] = C["TE"]
            te_settle_v()
            wait("vector", "TE", C["TE"])
            for half in range(2):
                bank = (t % 2) * 2 + half
                op("vector", lambda q, bank=bank, t=t, half=half:
                   q.tensor_copy(xqT3[:, half * 8:(half + 1) * 8, t * P:(t + 1) * P],
                                 psB3[bank][:, :, :]), sem=("V" if half == 1 else None))

    pipe_idx = [0]
    wq_te = [0, 0]
    pipe_v_slab = {}
    slab_guard = {}
    dstg_guard = [0, 0]

    def pipe_piece(mn, w_src_ap, dest_kind, dest, r, mi):
        gi = pipe_idx[0]; pipe_idx[0] += 1
        slot = gi % 2
        wtile_load(w_src_ap, slot, q=("sync" if slot == 0 else "scalar"))
        wait("scalar", f"LDW{slot}", C[f"LDW{slot}"])
        wait("scalar", "V", C["V"])
        wt = wtmp[gi % 2]
        op("scalar", lambda q, sl=slot, wt=wt, mi=mi:
           q.activation(wt, wtile.ap()[:, sl * D:(sl + 1) * D], AF.Identity,
                        bias=0.0, scale=rwsb.ap()[:, mi:mi + 1]), sem="S")
        wtile_cons[slot] = ("S", C["S"])
        pp = gi % 2
        wait("vector", "S", C["S"])
        wait("vector", f"CW{pp}", C[f"CW{pp}"])
        op("vector", lambda q, wt=wt, pp=pp:
           q.tensor_scalar(wqi8.ap()[:, pp * D:(pp + 1) * D], wt, 1.0, -1.0,
                           ALU.min, ALU.max), sem="V")
        wait("gpsimd", "V", C["V"])
        wait("gpsimd", "TE", wq_te[pp])
        dma("gpsimd", wqb_v[:, pp * D:(pp + 1) * D],
            wqi8.ap()[:, pp * D:(pp + 1) * D], f"CW{pp}")
        wait("tensor", f"CW{pp}", C[f"CW{pp}"])
        wait("tensor", "V", C["V"])
        for kk in range(KD):
            bank = (gi % 2) * 2 + kk // 8
            sl = kk % 8
            op("tensor", lambda q, pp=pp, kk=kk, bank=bank, sl=sl:
               q.transpose(psB3[bank][:, sl, :],
                           wqb_v[:, pp * D + kk * P: pp * D + (kk + 1) * P],
                           identb.ap()[:]), sem=("TE" if kk % 8 == 7 else None))
        wq_te[pp] = C["TE"]
        te_settle_v()
        wait("vector", "TE", C["TE"])
        if dest_kind == "slab":
            sloti = dest
            if r == 0 and sloti in slab_guard:
                te_v, spa_n, spa_v = slab_guard[sloti]
                wait("vector", "TE", max(te_v, C["TE"]))
                wait("vector", spa_n, spa_v)
            for half in range(2):
                bank = (gi % 2) * 2 + half
                op("vector", lambda q, bank=bank, sloti=sloti, r=r, half=half:
                   q.tensor_copy(slab3[sloti][:, half * 8:(half + 1) * 8, r * P:(r + 1) * P],
                                 psB3[bank][:, :, :]), sem=("V" if half == 1 else None))
        else:
            j, qq = dest
            dsl = gi % 2
            nm = f"SPD{dsl}"
            wait("vector", nm, C[nm])
            d3 = dstg_v[:, dsl * D:(dsl + 1) * D].rearrange("p (e c) -> p e c", e=16)
            for half in range(2):
                bank = (gi % 2) * 2 + half
                op("vector", lambda q, bank=bank, d3=d3, half=half:
                   q.tensor_copy(d3[:, half * 8:(half + 1) * 8, :], psB3[bank][:, :, :]),
                   sem=("V" if half == 1 else None))
            wait("sync", "V", C["V"])
            for k2 in range(16):
                dma("sync", RdT[:, qq * 16 + k2, j * P:(j + 1) * P],
                    d3[:, k2, :], nm)

    psA_free = [("S", 0, "V", 0)] * 2
    psB_free_y = [("V", 0)] * 4

    def ffn(src_h, resid_h, dst_h, produce, tag):
        phase_L(src_h, do_quant_stats=True)
        mark()
        op("vector", lambda q: q.tensor_scalar(sgat[:], sc_[:], wsb.ap()[:, 0:1], None, ALU.mult))
        op("vector", lambda q: q.tensor_scalar(supc[:], sc_[:], wsb.ap()[:, 1:2], None, ALU.mult))
        op("vector", lambda q: q.memset(hmaxc[:], 0.0), sem="V")
        dr()

        for oc in range(OCA):
            for mi, mn in ((0, "g"), (1, "u")):
                slotd = mi * 2 + oc % 2
                if produce:
                    w_ap = wg_in if mn == "g" else wu_in
                    for r in range(4):
                        j = oc * 4 + r
                        pipe_piece(mn, w_ap[j * P:(j + 1) * P, :], "slab", slotd, r, mi)
                    pipe_v_slab[(mn, oc)] = C["V"]
                    Rt = RgT if mn == "g" else RuT
                    spa = f"SPA{oc % 2}"
                    wait("sync", "V", C["V"])
                    for k2 in range(KD):
                        dma("sync", Rt[:, k2, oc * 512:(oc + 1) * 512],
                            slab3[slotd][:, k2, :], spa)
                else:
                    Rt = RgT if mn == "g" else RuT
                    sl_sem = f"SL{slotd}"
                    if (tag, slotd) in slab_guard:
                        te_v = slab_guard[(tag, slotd)]
                        wait("sync", "TE", te_v)
                    for k2 in range(KD):
                        dma("sync", slab3[slotd][:, k2, :],
                            Rt[:, k2, oc * 512:(oc + 1) * 512], sl_sem)
            if produce:
                for r in range(4):
                    pi = oc * 4 + r
                    j = pi // 4
                    qq = pi % 4
                    pipe_piece("d", wd_in[j * P:(j + 1) * P, qq * D:(qq + 1) * D],
                               "dstage", (j, qq), 0, 2)

            slot_g = 0 + oc % 2
            slot_u = 2 + oc % 2
            if produce:
                wait("tensor", "V", max(pipe_v_slab[("g", oc)], pipe_v_slab[("u", oc)]))
            else:
                wait("tensor", f"SL{slot_g}", C[f"SL{slot_g}"])
                wait("tensor", f"SL{slot_u}", C[f"SL{slot_u}"])
            for t in range(TT):
                pr = t % 2
                sN, sV_, vN, vV_ = psA_free[pr]
                wait("tensor", vN, vV_)
                wait("tensor", sN, sV_)
                pg = psA[pr * 2].ap()
                pu = psA[pr * 2 + 1].ap()
                for k in range(KD):
                    st = (k == 0); sp = (k == KD - 1)
                    op("tensor", lambda q, k=k, t=t, pg=pg, sg=slot_g, st=st, sp=sp:
                       q.matmul(pg[:, 0:512], xqT3[:, k, t * P:(t + 1) * P],
                                slab3[sg][:, k, :], start=st, stop=sp))
                    op("tensor", lambda q, k=k, t=t, pu=pu, su=slot_u, st=st, sp=sp:
                       q.matmul(pu[:, 0:512], xqT3[:, k, t * P:(t + 1) * P],
                                slab3[su][:, k, :], start=st, stop=sp),
                       sem=("TE" if sp else None))
                mm_done = C["TE"]
                eh = (oc * TT + t) % 2
                wait("scalar", "TE", mm_done)
                wait("scalar", "V", psA_free[pr][3])
                op("scalar", lambda q, pg=pg, t=t, eh=eh:
                   q.activation(hgb.ap()[:, eh * 512:(eh + 1) * 512], pg[:, 0:512],
                                AF.Silu, bias=0.0, scale=sgat.ap()[:, t:t + 1]), sem="S")
                s_done = C["S"]
                wait("vector", "S", s_done)
                wait("vector", f"HS{eh}", C[f"HS{eh}"])
                op("vector", lambda q, pu=pu, t=t, eh=eh:
                   q.scalar_tensor_tensor(hob.ap()[:, eh * 512:(eh + 1) * 512], pu[:, 0:512],
                                          supc.ap()[:, t:t + 1],
                                          hgb.ap()[:, eh * 512:(eh + 1) * 512],
                                          ALU.mult, ALU.mult))
                dr()
                op("vector", lambda q, eh=eh:
                   q.tensor_reduce(tmp1c.ap()[:, 0:1], hob.ap()[:, eh * 512:(eh + 1) * 512],
                                   AX.X, ALU.max, apply_absolute_value=True))
                dr()
                op("vector", lambda q, t=t:
                   q.tensor_tensor(hmaxc.ap()[:, t:t + 1], hmaxc.ap()[:, t:t + 1],
                                   tmp1c.ap()[:, 0:1], ALU.max), sem="V")
                v_done = C["V"]
                psA_free[pr] = ("S", s_done, "V", v_done)
                wait("scalar", "V", v_done)
                dma("scalar", h_h[t * P:(t + 1) * P, oc * 512:(oc + 1) * 512],
                    hob.ap()[:, eh * 512:(eh + 1) * 512], f"HS{eh}")
            if not produce:
                slab_guard[(tag, slot_g)] = C["TE"]
                slab_guard[(tag, slot_u)] = C["TE"]
            else:
                slab_guard[slot_g] = (C["TE"], f"SPA{oc % 2}", C[f"SPA{oc % 2}"])
                slab_guard[slot_u] = (C["TE"], f"SPA{oc % 2}", C[f"SPA{oc % 2}"])

        mark()
        wait("vector", "V", C["V"])
        dr()
        op("vector", lambda q: q.tensor_scalar(s2c[:], hmaxc[:], INV127, EPS,
                                               ALU.mult, ALU.add))
        recip(lambda q: q.reciprocal(rs2c[:], s2c[:]), "rs2")
        op("vector", lambda q: q.tensor_scalar(scoc[:], s2c[:], wsb.ap()[:, 2:3], 0.5,
                                               ALU.mult, ALU.mult), sem="V")
        dr()
        V_H = C["V"]

        HS_ALL0, HS_ALL1 = C["HS0"], C["HS1"]

        for half in range(2):
            if half == 1 and BARRIER:
                for e in ("scalar", "vector", "tensor", "gpsimd", "sync"):
                    wait(e, "TE", C["TE"])
                    wait(e, "V", C["V"])
                    wait(e, "S", C["S"])
                    wait(e, "YW0", C["YW0"])
                    wait(e, "YW1", C["YW1"])
            for tl in range(4):
                t = half * 4 + tl
                for hc in range(4):
                    ci = (t * 4 + hc) % 2
                    slot = ci
                    if wtile_cons[slot] is not None:
                        nm, vv = wtile_cons[slot]
                        wait("scalar", nm, vv)
                    if half == 0 and tl == 0 and hc == 0:
                        wait("scalar", "HS0", HS_ALL0)
                        wait("scalar", "HS1", HS_ALL1)
                    dma("scalar", wtile.ap()[:, slot * D:(slot + 1) * D],
                        h_h[t * P:(t + 1) * P, hc * D:(hc + 1) * D], f"LDW{slot}")
                    wait("scalar", f"LDW{slot}", C[f"LDW{slot}"])
                    wait("scalar", "V", V_H)
                    pp = ci
                    wait("scalar", f"CX{pp}", C[f"CX{pp}"])
                    op("scalar", lambda q, sl=slot, t=t, pp=pp:
                       q.activation(xqi8.ap()[:, pp * D:(pp + 1) * D],
                                    wtile.ap()[:, sl * D:(sl + 1) * D], AF.Identity,
                                    bias=0.0, scale=rs2c.ap()[:, t:t + 1]), sem="S")
                    wtile_cons[slot] = ("S", C["S"])
                    wait("gpsimd", "S", C["S"])
                    wait("gpsimd", "TE", xq_te[pp])
                    dma("gpsimd", xqb.ap()[:, pp * D:(pp + 1) * D],
                        xqi8.ap()[:, pp * D:(pp + 1) * D], f"CX{pp}")
                    wait("tensor", f"CX{pp}", C[f"CX{pp}"])
                    wait("tensor", "V", C["V"])
                    for kk in range(KD):
                        bank = ci * 2 + kk // 8
                        sl2 = kk % 8
                        op("tensor", lambda q, pp=pp, kk=kk, bank=bank, sl2=sl2:
                           q.transpose(psA3[bank][:, sl2, :],
                                       xqb.ap()[:, pp * D + kk * P: pp * D + (kk + 1) * P],
                                       identb.ap()[:]), sem=("TE" if kk % 8 == 7 else None))
                    xq_te[pp] = C["TE"]
                    te_settle_v()
                    wait("vector", "TE", C["TE"])
                    for hh in range(2):
                        bank = ci * 2 + hh
                        kbase = hc * 16 + hh * 8
                        op("vector", lambda q, bank=bank, kbase=kbase, tl=tl:
                           q.tensor_copy(hqT3[:, kbase:kbase + 8, tl * P:(tl + 1) * P],
                                         psA3[bank][:, :, :]), sem=("V" if hh == 1 else None))
            V_HQ = C["V"]
            mark()
            for oc in range(OCB):
                for kh in range(2):
                    Q = "sync" if kh == 0 else "gpsimd"
                    nm = f"WD{kh}"
                    if ("wd", kh) in slab_guard:
                        wait(Q, "TE", slab_guard[("wd", kh)])
                    if oc == 0 and half == 0:
                        wait(Q, "TE", C["TE"])
                        if kh == 0:
                            wait(Q, "S", C["S"])
                            wait(Q, "V", C["V"])
                            wait(Q, "SPA0", C["SPA0"])
                            wait(Q, "SPA1", C["SPA1"])
                            if produce:
                                wait(Q, "SPD0", C["SPD0"])
                                wait(Q, "SPD1", C["SPD1"])
                    for k2 in range(32):
                        dma(Q, wdsl3[kh][:, k2, :],
                            RdT[:, kh * 32 + k2, oc * 256:(oc + 1) * 256], nm)
                wait("tensor", "V", V_HQ)
                for kh in range(2):
                    wait("tensor", f"WD{kh}", C[f"WD{kh}"])
                    for tl in range(4):
                        if kh == 0:
                            nmv, vv = psB_free_y[tl]
                            wait("tensor", nmv, vv)
                        for kk in range(32):
                            k = kh * 32 + kk
                            st = (k == 0); sp = (k == KH - 1)
                            op("tensor", lambda q, tl=tl, k=k, kk=kk, kh=kh, st=st, sp=sp:
                               q.matmul(psB[tl].ap()[:, 0:256],
                                        hqT3[:, k, tl * P:(tl + 1) * P],
                                        wdsl3[kh][:, kk, :], start=st, stop=sp),
                               sem=("TE" if (sp or kk == 31) else None))
                    slab_guard[("wd", kh)] = C["TE"]
                for tl in range(4):
                    t = half * 4 + tl
                    eh = (oc * 4 + tl) % 2
                    mm_done = C["TE"] - (3 - tl)
                    wait("scalar", "TE", mm_done)
                    wait("scalar", f"RS{eh}", C[f"RS{eh}"])
                    wait("scalar", "V", C["V"])
                    dma("scalar", rsb.ap()[:, eh * 512:eh * 512 + 256],
                        resid_h[t * P:(t + 1) * P, oc * 256:(oc + 1) * 256], f"RS{eh}")
                    wait("vector", f"RS{eh}", C[f"RS{eh}"])
                    wait("vector", "TE", mm_done)
                    wait("vector", f"YW{eh}", C[f"YW{eh}"])
                    op("vector", lambda q, tl=tl, t=t, eh=eh:
                       q.scalar_tensor_tensor(yob.ap()[:, eh * 512:eh * 512 + 256],
                                              psB[tl].ap()[:, 0:256],
                                              scoc.ap()[:, t:t + 1],
                                              rsb.ap()[:, eh * 512:eh * 512 + 256],
                                              ALU.mult, ALU.add), sem="V")
                    psB_free_y[tl] = ("V", C["V"])
                    wait("scalar", "V", C["V"])
                    dma("scalar", dst_h[t * P:(t + 1) * P, oc * 256:(oc + 1) * 256],
                        yob.ap()[:, eh * 512:eh * 512 + 256], f"YW{eh}")
                mark()
        mark("ffn_end")

    def attn():
        wait("scalar", "YW0", C["YW0"])
        wait("scalar", "YW1", C["YW1"])
        wait("sync", "TE", C["TE"])
        dma("sync", wcmv[0:64, :], wcm_in, "SM")
        dma("sync", dbc[0:1, :], dv_in, "SM")
        SM_W = C["SM"]
        phase_L(x1_h, do_quant_stats=False)
        V_ST = C["V"]
        mark()
        wait("tensor", "SM", SM_W)
        for ocq in range(4):
            bank = ocq % 2
            wait("tensor", "S", C["S"])
            wait("tensor", "V", C["V"])
            op("tensor", lambda q, ocq=ocq, bank=bank:
               q.matmul(psB[bank].ap()[:, 0:512], onesrow.ap()[:],
                        dbc[0:1, ocq * 512:(ocq + 1) * 512],
                        start=True, stop=True), sem="TE")
            te_settle_s()
            wait("scalar", "TE", C["TE"])
            op("scalar", lambda q, ocq=ocq, bank=bank:
               q.copy(dbc[:, ocq * 512:(ocq + 1) * 512], psB[bank].ap()[:, 0:512]), sem="S")
        mark()
        for t in range(TT):
            slot = t % 2
            xt_load(x1_h[t * P:(t + 1) * P, :], slot)
            wait("vector", f"XT{slot}", C[f"XT{slot}"])
            wait("vector", "V", V_ST)
            wait("vector", "S", C["S"])
            op("vector", lambda q, sl=slot, t=t:
               q.tensor_scalar(sqscr, xt.ap()[:, sl * D:(sl + 1) * D],
                               mcol.ap()[:, t:t + 1], None, ALU.subtract), sem="V")
            xt_cons[slot] = ("V", C["V"])
            wait("tensor", "V", C["V"])
            for kk in range(KD):
                bank = kk // 4
                sl2 = kk % 4
                op("tensor", lambda q, kk=kk, bank=bank, sl2=sl2:
                   q.transpose(psAf3[bank][:, sl2, :], sqscr[:, kk * P:(kk + 1) * P],
                               identf.ap()[:]), sem=("TE" if kk == KD - 1 else None))
            te_settle_v()
            wait("vector", "TE", C["TE"])
            for bank in range(4):
                op("vector", lambda q, bank=bank, t=t:
                   q.tensor_copy(xcT3[:, bank * 4:(bank + 1) * 4, t * P:(t + 1) * P],
                                 psAf3[bank][:, :, :]), sem=("V" if bank == 3 else None))
        mark()
        wait("tensor", "SM", SM_W)
        for tc in range(2):
            wait("tensor", "V", C["V"])
            wait("tensor", "S", C["S"])
            for k in range(KD):
                st = (k == 0); sp = (k == KD - 1)
                op("tensor", lambda q, k=k, tc=tc, st=st, sp=sp:
                   q.matmul(psA[2].ap()[0:64, 0:512], wleft.ap()[:, k * 64:(k + 1) * 64],
                            xcT3[:, k, tc * 512:(tc + 1) * 512], start=st, stop=sp),
                   sem=("TE" if sp else None))
            te_settle_s()
            wait("scalar", "TE", C["TE"])
            op("scalar", lambda q, tc=tc:
               q.copy(t1Tv[0:64, tc * 512:(tc + 1) * 512], psA[2].ap()[0:64, 0:512]), sem="S")
        S_T1T = C["S"]
        mark()
        y_guard = {2: ("V", C["V"]), 3: ("V", C["V"])}
        for t in range(TT):
            for ocq in range(4):
                bank = 2 + (t * 4 + ocq) % 2
                wait("tensor", "S", S_T1T)
                nmv, vv = y_guard[bank]
                wait("tensor", nmv, vv)
                op("tensor", lambda q, t=t, ocq=ocq, bank=bank:
                   q.matmul(psA[bank].ap()[:, 0:512], t1Tv[0:64, t * P:(t + 1) * P],
                            wcmv[0:64, ocq * 512:(ocq + 1) * 512],
                            start=True, stop=True), sem="TE")
                mm_done = C["TE"]
                eh = (t * 4 + ocq) % 2
                wait("scalar", f"RS{eh}", C[f"RS{eh}"])
                wait("scalar", "V", C["V"])
                dma("scalar", rsb.ap()[:, eh * 512:(eh + 1) * 512],
                    x1_h[t * P:(t + 1) * P, ocq * 512:(ocq + 1) * 512], f"RS{eh}")
                wait("vector", f"RS{eh}", C[f"RS{eh}"])
                wait("vector", "TE", mm_done)
                wait("vector", f"YW{eh}", C[f"YW{eh}"])
                op("vector", lambda q, t=t, eh=eh, ocq=ocq:
                   q.scalar_tensor_tensor(hgb.ap()[:, eh * 512:(eh + 1) * 512],
                                          rsb.ap()[:, eh * 512:(eh + 1) * 512],
                                          mcol.ap()[:, t:t + 1],
                                          dbc[:, ocq * 512:(ocq + 1) * 512],
                                          ALU.subtract, ALU.mult))
                dr()
                op("vector", lambda q, eh=eh, bank=bank:
                   q.tensor_tensor(hob.ap()[:, eh * 512:(eh + 1) * 512],
                                   hgb.ap()[:, eh * 512:(eh + 1) * 512],
                                   psA[bank].ap()[:, 0:512], ALU.add))
                dr()
                op("vector", lambda q, t=t, eh=eh:
                   q.scalar_tensor_tensor(yob.ap()[:, eh * 512:(eh + 1) * 512],
                                          hob.ap()[:, eh * 512:(eh + 1) * 512],
                                          rstdc.ap()[:, t:t + 1],
                                          rsb.ap()[:, eh * 512:(eh + 1) * 512],
                                          ALU.mult, ALU.add), sem="V")
                y_guard[bank] = ("V", C["V"])
                wait("scalar", "V", C["V"])
                dma("scalar", x2_h[t * P:(t + 1) * P, ocq * 512:(ocq + 1) * 512],
                    yob.ap()[:, eh * 512:(eh + 1) * 512], f"YW{eh}")

    ffn(u_in, u_in, x1_h, produce=True, tag=1)
    attn()
    mark("attn_end")
    wait("scalar", "YW0", C["YW0"])
    wait("scalar", "YW1", C["YW1"])
    wait("sync", "SPA0", C["SPA0"])
    wait("sync", "SPA1", C["SPA1"])
    wait("sync", "SPD0", C["SPD0"])
    wait("sync", "SPD1", C["SPD1"])
    wait("sync", "TE", C["TE"])
    wait("sync", "V", C["V"])
    ffn(x2_h, x2_h, out_d, produce=False, tag=2)

    wait("gpsimd", "YW0", C["YW0"])
    wait("gpsimd", "YW1", C["YW1"])
    mark()
    build.last_marks = marks

    if isinstance(phases, str):
        if phases == "ffn1_end":
            phases = [i for i, m in enumerate(marks) if m[2] == "ffn_end"][0] + 1
        elif phases == "ffn2_end":
            phases = [i for i, m in enumerate(marks) if m[2] == "ffn_end"][1] + 1
        elif phases == "attn_end":
            phases = [i for i, m in enumerate(marks) if m[2] == "attn_end"][0] + 1
    if phases is not None and phases < len(marks):
        cut, cutC, _lbl = marks[phases]
        for k in prog:
            prog[k] = prog[k][:cut[k]]
    else:
        cutC = dict(C)
    if dbg is not None:
        for nm, vv in cutC.items():
            p("gpsimd", lambda q, s_=S[nm], v_=vv: q.wait_ge(s_, v_))
        ndma = 0
        if dbg == "arena0":
            for c2 in range(8):
                p("gpsimd", lambda q, c2=c2: q.dma_start(
                    out=out_d[c2 * 128:(c2 + 1) * 128, :],
                    in_=AFp[:, c2 * 2048:(c2 + 1) * 2048]).then_inc(S["SM"], 16))
                ndma += 1
        elif dbg == "arena1":
            for c2 in range(8):
                p("gpsimd", lambda q, c2=c2: q.dma_start(
                    out=out_d[c2 * 128:(c2 + 1) * 128, :],
                    in_=AFp[:, 16384 + c2 * 2048:16384 + (c2 + 1) * 2048]).then_inc(S["SM"], 16))
                ndma += 1
        elif dbg in ("x1", "x2", "hq0"):
            srcT = {"x1": x1_h, "x2": x2_h}.get(dbg)
            if dbg == "hq0":
                srcT = h_h[:, 0:2048]
            for c2 in range(8):
                p("gpsimd", lambda q, c2=c2, srcT=srcT: q.dma_start(
                    out=out_d[c2 * 128:(c2 + 1) * 128, :],
                    in_=srcT[c2 * 128:(c2 + 1) * 128, :]).then_inc(S["SM"], 16))
                ndma += 1
        elif dbg == "wgT":
            bt = wgT_h.bitcast(F32)
            for c2 in range(8):
                p("gpsimd", lambda q, c2=c2, bt=bt: q.dma_start(
                    out=out_d[c2 * 128:(c2 + 1) * 128, :],
                    in_=bt[c2 * 128:(c2 + 1) * 128, 0:2048]).then_inc(S["SM"], 16))
                ndma += 1
        elif dbg == "stats":
            dumps = [wsb, rwsb, sc_, rsc, rstdc, qqc, nmqc, sgat, supc, hmaxc,
                     s2c, rs2c, scoc, mcol, ssqc, vc, sdc, wsraw, amaxc, sumc]
            for i2, tens in enumerate(dumps):
                w2 = tens.ap().shape[1]
                p("gpsimd", lambda q, i2=i2, tens=tens, w2=w2: q.dma_start(
                    out=out_d[0:128, i2 * 8:i2 * 8 + w2],
                    in_=tens.ap()[:, 0:w2]).then_inc(S["SM"], 16))
                ndma += 1
        elif dbg == "wdT":
            bt = wdT_h.bitcast(F32)
            for c2 in range(8):
                p("gpsimd", lambda q, c2=c2, bt=bt: q.dma_start(
                    out=out_d[c2 * 128:(c2 + 1) * 128, 0:1024],
                    in_=bt[c2 * 128:(c2 + 1) * 128, 0:1024]).then_inc(S["SM"], 16))
                ndma += 1
        tot = cutC["SM"] + 16 * ndma
        p("gpsimd", lambda q, tot=tot: q.wait_ge(S["SM"], tot))

    with es, nc.Block() as blk:
        @blk.sync
        def _(q):
            for f in prog["sync"]:
                f(q)

        @blk.scalar
        def _(q):
            for f in prog["scalar"]:
                f(q)

        @blk.vector
        def _(q):
            for f in prog["vector"]:
                f(q)

        @blk.tensor
        def _(q):
            for f in prog["tensor"]:
                f(q)

        @blk.gpsimd
        def _(q):
            for f in prog["gpsimd"]:
                f(q)
    return nc


_NC_CACHE = []


def kernel(L0_attn_u, L0_attn_A_factors, L0_attn_W_b_rank, L0_attn_W_b_state,
           L0_attn_W_c_rank, L0_attn_W_c_model, L0_attn_D_vec,
           L0_ffn_w_gate, L0_ffn_w_up, L0_ffn_w_down):
    u = np.ascontiguousarray(np.asarray(L0_attn_u, dtype=np.float32).reshape(8192, 2048))
    wg = np.ascontiguousarray(np.asarray(L0_ffn_w_gate, dtype=np.float32))
    wu = np.ascontiguousarray(np.asarray(L0_ffn_w_up, dtype=np.float32))
    wd = np.ascontiguousarray(np.asarray(L0_ffn_w_down, dtype=np.float32))
    a_fl = np.ascontiguousarray(np.asarray(L0_attn_A_factors, dtype=np.float32).reshape(1, 32))
    wbr = np.ascontiguousarray(np.asarray(L0_attn_W_b_rank, dtype=np.float32))
    wbs = np.ascontiguousarray(np.asarray(L0_attn_W_b_state, dtype=np.float32))
    wcr = np.ascontiguousarray(np.asarray(L0_attn_W_c_rank, dtype=np.float32))
    wcm = np.ascontiguousarray(np.asarray(L0_attn_W_c_model, dtype=np.float32))
    dv = np.ascontiguousarray(np.asarray(L0_attn_D_vec, dtype=np.float32).reshape(1, 2048))
    idb = np.eye(P, dtype=np.float32).astype(ml_dtypes.bfloat16)
    idf = np.eye(P, dtype=np.float32)
    onc = np.ones((P, 1), dtype=np.float32)
    onr = np.ones((1, P), dtype=np.float32)

    if not _NC_CACHE:
        _NC_CACHE.append(build())
    nc = _NC_CACHE[0]

    in_maps = []
    for c in range(8):
        in_maps.append(dict(u=np.ascontiguousarray(u[c * TOK:(c + 1) * TOK]),
                            wg=wg, wu=wu, wd=wd, a_fl=a_fl, wbr=wbr, wbs=wbs,
                            wcr=wcr, wcm=wcm, dv=dv, idb=idb, idf=idf,
                            onc=onc, onr=onr))
    res = run_bass_kernel_spmd(nc, in_maps, core_ids=list(range(8)))
    out = np.concatenate([res.results[c]["out"] for c in range(8)], axis=0)
    return out.reshape(4, 2048, 2048)


if __name__ == "__main__":
    nc = build()
    print("build OK")


# version 05 (final)
# speedup vs baseline: 1.0827x; 1.0071x over previous
import numpy as np
import ml_dtypes
import contextlib
import concourse.bass as bass
import concourse.mybir as mybir
from concourse.bass_utils import run_bass_kernel_spmd

F32 = mybir.dt.float32
BF16 = mybir.dt.bfloat16
I8 = mybir.dt.int8
AF = mybir.ActivationFunctionType
ALU = mybir.AluOpType
AX = mybir.AxisListType

P = 128
TOK = 1024
TT = 8
D = 2048
KD = 16
H = 8192
KH = 64
OCA = 16
OCB = 8
INV127 = 1.0 / 127.0
INV2048 = 1.0 / 2048.0
INV2_24 = 1.0 / float(1 << 24)
EPS = 1e-5
BARRIER = False


def build(phases=None, dbg=None):
    nc = bass.Bass("TRN2", target_bir_lowering=False)

    u_in = nc.dram_tensor("u", [TOK, D], F32, kind="ExternalInput").ap()
    wg_in = nc.dram_tensor("wg", [H, D], F32, kind="ExternalInput").ap()
    wu_in = nc.dram_tensor("wu", [H, D], F32, kind="ExternalInput").ap()
    wd_in = nc.dram_tensor("wd", [D, H], F32, kind="ExternalInput").ap()
    a_fl = nc.dram_tensor("a_fl", [1, 32], F32, kind="ExternalInput").ap()
    wbr_in = nc.dram_tensor("wbr", [D, 64], F32, kind="ExternalInput").ap()
    wbs_in = nc.dram_tensor("wbs", [64, 256], F32, kind="ExternalInput").ap()
    wcr_in = nc.dram_tensor("wcr", [256, 64], F32, kind="ExternalInput").ap()
    wcm_in = nc.dram_tensor("wcm", [64, D], F32, kind="ExternalInput").ap()
    dv_in = nc.dram_tensor("dv", [1, D], F32, kind="ExternalInput").ap()
    idb_in = nc.dram_tensor("idb", [P, P], BF16, kind="ExternalInput").ap()
    idf_in = nc.dram_tensor("idf", [P, P], F32, kind="ExternalInput").ap()
    onc_in = nc.dram_tensor("onc", [P, 1], F32, kind="ExternalInput").ap()
    onr_in = nc.dram_tensor("onr", [1, P], F32, kind="ExternalInput").ap()

    out_d = nc.dram_tensor("out", [TOK, D], F32, kind="ExternalOutput").ap()

    wgT_h = nc.dram_tensor("wgT_h", [D, H], BF16, kind="Internal").ap()
    wuT_h = nc.dram_tensor("wuT_h", [D, H], BF16, kind="Internal").ap()
    wdT_h = nc.dram_tensor("wdT_h", [H, D], BF16, kind="Internal").ap()
    h_h = nc.dram_tensor("h_h", [TOK, H], F32, kind="Internal").ap()
    x1_h = nc.dram_tensor("x1_h", [TOK, D], F32, kind="Internal").ap()
    x2_h = nc.dram_tensor("x2_h", [TOK, D], F32, kind="Internal").ap()

    RgT = wgT_h.rearrange("(k p) o -> p k o", p=P)
    RuT = wuT_h.rearrange("(k p) o -> p k o", p=P)
    RdT = wdT_h.rearrange("(k p) o -> p k o", p=P)

    es = contextlib.ExitStack()
    sb = lambda name, shp, dt: es.enter_context(nc.sbuf_tensor(name, shp, dt))

    arena = sb("arena", [P, 32768], F32)
    AFp = arena.ap()
    ABp = AFp.bitcast(BF16)

    def fa(b0, b1):
        return AFp[:, b0 // 4: b1 // 4]

    def fb(b0, b1):
        return ABp[:, b0 // 2: b1 // 2]

    xqT3 = fb(0, 32768).rearrange("p (k t) -> p k t", k=KD)
    slab3 = [fb(32768 + i * 16384, 32768 + (i + 1) * 16384)
             .rearrange("p (k c) -> p k c", k=KD) for i in range(4)]
    hqT3 = fb(0, 65536).rearrange("p (k c) -> p k c", k=KH)
    wdsl3 = [fb(65536 + i * 16384, 65536 + (i + 1) * 16384)
             .rearrange("p (k c) -> p k c", k=32) for i in range(2)]
    wtmp = [fa(98304 + i * 8192, 98304 + (i + 1) * 8192) for i in range(2)]
    sqscr = fa(114688, 122880)
    wqb_v = fb(114688, 122880)
    dstg_v = fb(122880, 131072)
    xcT3 = fa(0, 65536).rearrange("p (k t) -> p k t", k=KD)
    dbc = fa(65536, 73728)
    wcmv = fa(73728, 81920)
    t1Tv = fa(81920, 86016)

    xt = sb("xt", [P, 2 * D], F32)
    wtile = sb("wtile", [P, 2 * D], F32)
    xqi8 = sb("xqi8", [P, 2 * D], I8)
    xqb = sb("xqb", [P, 2 * D], BF16)
    wqi8 = sb("wqi8", [P, 2 * D], I8)
    hgb = sb("hgb", [P, 2 * 512], F32)
    hob = sb("hob", [P, 2 * 512], F32)
    yob = sb("yob", [P, 2 * 512], F32)
    rsb = sb("rsb", [P, 2 * 512], F32)
    wleft = sb("wleft", [P, KD * 64], F32)
    redc = sb("redc", [P, 3 * 64], F32)
    identb = sb("identb", [P, P], BF16)
    identf = sb("identf", [P, P], F32)
    onescol = sb("onescol", [P, 1], F32)
    onesrow = sb("onesrow", [1, P], F32)
    a_bc = sb("a_bc", [64, 32], F32)
    wbs_a = sb("wbs_a", [64, 256], F32)
    wbs_b = sb("wbs_b", [64, 256], F32)
    btmp = sb("btmp", [64, 128], F32)
    wbsmT = sb("wbsmT", [P, P], F32)
    wmid = sb("wmid", [64, 64], F32)
    wbrT1 = sb("wbrT1", [64, P], F32)
    wbr_sb = sb("wbr_sb", [P, KD * 64], F32)
    wcr_sb = sb("wcr_sb", [P, 2 * 64], F32)
    totsb = sb("totsb", [1, 4], F32)

    def stat(name):
        return sb(name, [P, 8], F32)

    sumc = stat("sumc"); xmaxc = stat("xmaxc"); xminc = stat("xminc")
    ssqc = stat("ssqc"); mcol = stat("mcol"); nmcol = stat("nmcol")
    vc = stat("vc"); sdc = stat("sdc"); rstdc = stat("rstdc")
    amax1 = stat("amax1"); amax2 = stat("amax2"); amaxc = stat("amaxc")
    sc_ = stat("sc_"); rsc = stat("rsc"); qqc = stat("qqc"); nmqc = stat("nmqc")
    sgat = stat("sgat"); supc = stat("supc")
    hmaxc = stat("hmaxc"); s2c = stat("s2c"); rs2c = stat("rs2c"); scoc = stat("scoc")
    tmp1c = stat("tmp1c")
    wsraw = sb("wsraw", [P, 3], F32)
    wsb = sb("wsb", [P, 3], F32)
    rwsb = sb("rwsb", [P, 3], F32)

    psA = [es.enter_context(nc.psum_tensor(f"psA{i}", [P, 512], F32)) for i in range(4)]
    psB = [es.enter_context(nc.psum_tensor(f"psB{i}", [P, 512], F32)) for i in range(4)]
    psAb = [t.ap().bitcast(BF16) for t in psA]
    psBb = [t.ap().bitcast(BF16) for t in psB]
    psA3 = [t.ap().bitcast(BF16).rearrange("p (e c) -> p e c", e=8) for t in psA]
    psB3 = [t.ap().bitcast(BF16).rearrange("p (e c) -> p e c", e=8) for t in psB]
    psAf3 = [t.ap().rearrange("p (e c) -> p e c", e=4) for t in psA]

    sem_names = ["SM", "LDW0", "LDW1", "XT0", "XT1", "CW0", "CW1", "CX0", "CX1",
                 "SL0", "SL1", "SL2", "SL3", "WD0", "WD1", "HS0", "HS1",
                 "RS0", "RS1", "YW0", "YW1", "SPA0", "SPA1", "SPD0", "SPD1",
                 "V", "S", "TE"]
    S = {n: es.enter_context(nc.semaphore(n)) for n in sem_names}
    C = {n: 0 for n in sem_names}

    prog = {k: [] for k in ("sync", "scalar", "vector", "tensor", "gpsimd")}
    marks = []

    def mark(label=None):
        marks.append(({k: len(v) for k, v in prog.items()}, dict(C), label))

    def p(eng, fn):
        prog[eng].append(fn)

    def wait(eng, name, val=None):
        v = C[name] if val is None else val
        p(eng, lambda q, s=S[name], vv=v: q.wait_ge(s, vv))

    def dma(eng, out, in_, sem):
        p(eng, lambda q, o=out, i=in_, s=S[sem]: q.dma_start(out=o, in_=i).then_inc(s, 16))
        C[sem] += 16

    def op(eng, fn, sem=None):
        if sem is None:
            p(eng, fn)
        else:
            p(eng, lambda q, f=fn, s=S[sem]: f(q).then_inc(s, 1))
            C[sem] += 1

    def dr():
        p("vector", lambda q: q.drain())

    def vs_fence():
        op("vector", lambda q: q.tensor_copy(totsb.ap()[0:1, 3:4], totsb.ap()[0:1, 2:3]),
           sem="V")
        wait("scalar", "V", C["V"])
        op("scalar", lambda q: q.copy(totsb.ap()[0:1, 2:3], totsb.ap()[0:1, 3:4]), sem="S")
        wait("vector", "S", C["S"])

    def recip(dst_fn, src_desc):
        vs_fence()
        op("vector", dst_fn, sem="V")
        vs_fence()

    def te_settle_v():
        wait("scalar", "TE", C["TE"])
        op("scalar", lambda q: q.copy(totsb.ap()[0:1, 1:2], totsb.ap()[0:1, 0:1]), sem="S")
        wait("vector", "S", C["S"])

    def te_settle_s():
        wait("vector", "TE", C["TE"])
        op("vector", lambda q: q.tensor_copy(totsb.ap()[0:1, 1:2], totsb.ap()[0:1, 0:1]),
           sem="V")
        wait("scalar", "V", C["V"])

    dma("sync", identb[:], idb_in, "SM")
    dma("sync", identf[:], idf_in, "SM")
    dma("sync", onescol[:], onc_in, "SM")
    dma("sync", onesrow[:], onr_in, "SM")
    dma("sync", a_bc[:], a_fl.partition_broadcast(64), "SM")
    dma("sync", wbs_a[:], wbs_in, "SM")
    dma("sync", wcr_sb[:, 0:64], wcr_in[0:128, :], "SM")
    dma("sync", wcr_sb[:, 64:128], wcr_in[128:256, :], "SM")
    for k2 in range(KD):
        dma("sync", wbr_sb.ap()[:, k2 * 64:(k2 + 1) * 64],
            wbr_in[k2 * P:(k2 + 1) * P, :], "SM")
    SM_ALL = C["SM"]

    wait("vector", "SM", SM_ALL)
    for lvl in range(8):
        src = (wbs_a if lvl % 2 == 0 else wbs_b).ap()
        dst = (wbs_b if lvl % 2 == 0 else wbs_a).ap()
        pre = 1 << lvl
        post = 128 >> lvl
        sv = src.rearrange("p (pre two post) -> p pre two post", pre=pre, two=2)
        dvv = dst.rearrange("p (pre two post) -> p pre two post", pre=pre, two=2)
        a00 = a_bc.ap()[:, 4 * lvl + 0: 4 * lvl + 1]
        a01 = a_bc.ap()[:, 4 * lvl + 1: 4 * lvl + 2]
        a10 = a_bc.ap()[:, 4 * lvl + 2: 4 * lvl + 3]
        a11 = a_bc.ap()[:, 4 * lvl + 3: 4 * lvl + 4]
        tb = btmp.ap()[:, 0:128].rearrange("p (pre post) -> p pre post", pre=pre)
        last = (lvl == 7)
        dr()
        op("vector", lambda q, o=tb, i=sv, a=a00: q.tensor_scalar(o, i[:, :, 0, :], a, None, ALU.mult))
        dr()
        op("vector", lambda q, o=dvv, i=sv, a=a10, t=tb:
           q.scalar_tensor_tensor(o[:, :, 0, :], i[:, :, 1, :], a, t, ALU.mult, ALU.add))
        dr()
        op("vector", lambda q, o=tb, i=sv, a=a01: q.tensor_scalar(o, i[:, :, 0, :], a, None, ALU.mult))
        dr()
        op("vector", lambda q, o=dvv, i=sv, a=a11, t=tb:
           q.scalar_tensor_tensor(o[:, :, 1, :], i[:, :, 1, :], a, t, ALU.mult, ALU.add),
           sem="V" if last else None)
    V_BFLY = C["V"]
    wbs_fin = wbs_b
    wbs_fin = wbs_a if (8 % 2 == 0) else wbs_b
    wbs_fin = wbs_a

    wait("tensor", "V", V_BFLY)
    op("tensor", lambda q: q.transpose(psB[0].ap()[0:128, 0:64], wbs_fin.ap()[:, 0:128],
                                       identf.ap()[0:64, 0:64]))
    op("tensor", lambda q: q.transpose(psB[0].ap()[0:128, 64:128], wbs_fin.ap()[:, 128:256],
                                       identf.ap()[0:64, 0:64]), sem="TE")
    TE_T1 = C["TE"]
    te_settle_s()
    wait("scalar", "TE", TE_T1)
    op("scalar", lambda q: q.copy(wbsmT[:], psB[0].ap()[0:128, 0:128]), sem="S")
    S_T1 = C["S"]
    wait("tensor", "S", S_T1)
    op("tensor", lambda q: q.matmul(psB[1].ap()[0:64, 0:64], wbsmT.ap()[:, 0:64],
                                    wcr_sb.ap()[:, 0:64], start=True, stop=False))
    op("tensor", lambda q: q.matmul(psB[1].ap()[0:64, 0:64], wbsmT.ap()[:, 64:128],
                                    wcr_sb.ap()[:, 64:128], start=False, stop=True), sem="TE")
    TE_T2 = C["TE"]
    te_settle_s()
    wait("scalar", "TE", TE_T2)
    op("scalar", lambda q: q.copy(wmid[:], psB[1].ap()[0:64, 0:64]), sem="S")
    S_T2 = C["S"]
    wait("tensor", "S", S_T2)
    for k in range(KD):
        wait("tensor", "S", C["S"])
        op("tensor", lambda q, k=k: q.transpose(psB[0].ap()[0:64, 0:128],
                                                wbr_sb.ap()[:, k * 64:(k + 1) * 64],
                                                identf.ap()[0:128, 0:128]), sem="TE")
        te_settle_s()
        wait("scalar", "TE", C["TE"])
        op("scalar", lambda q: q.copy(wbrT1[:], psB[0].ap()[0:64, 0:128]), sem="S")
        wait("tensor", "S", C["S"])
        op("tensor", lambda q: q.matmul(psB[1].ap()[0:128, 0:64], wbrT1.ap()[:],
                                        wmid.ap()[:], start=True, stop=True), sem="TE")
        te_settle_s()
        wait("scalar", "TE", C["TE"])
        op("scalar", lambda q, k=k: q.copy(wleft.ap()[:, k * 64:(k + 1) * 64],
                                           psB[1].ap()[0:128, 0:64]), sem="S")
    S_SETUP = C["S"]
    mark()

    wtile_use = [0, 0]
    wtile_cons = [None, None]

    def wtile_load(src_ap, slot, q="sync"):
        if wtile_cons[slot] is not None:
            nm, vv = wtile_cons[slot]
            wait(q, nm, vv)
        dma(q, wtile.ap()[:, slot * D:(slot + 1) * D], src_ap, f"LDW{slot}")

    mats = [("g", wg_in, redc.ap()[:, 0:64]), ("u", wu_in, redc.ap()[:, 64:128]),
            ("d", wd_in, redc.ap()[:, 128:192])]
    for mi, (mn, w_ap, rcol) in enumerate(mats):
        for j in range(64):
            slot = (mi * 64 + j) % 2
            if mn == "d":
                src = w_ap[(j // 4) * P:(j // 4 + 1) * P, (j % 4) * D:(j % 4 + 1) * D]
            else:
                src = w_ap[j * P:(j + 1) * P, :]
            wtile_load(src, slot, q=("sync" if slot == 0 else "gpsimd"))
            wait("vector", f"LDW{slot}", C[f"LDW{slot}"])
            op("vector", lambda q, o=rcol, j=j, sl=slot:
               q.tensor_reduce(o[:, j:j + 1], wtile.ap()[:, sl * D:(sl + 1) * D],
                               AX.X, ALU.add, apply_absolute_value=True), sem="V")
            wtile_cons[slot] = ("V", C["V"])
    for mi in range(3):
        rcol = redc.ap()[:, mi * 64:(mi + 1) * 64]
        dr()
        op("vector", lambda q, r=rcol, mi=mi:
           q.tensor_reduce(wsraw.ap()[:, mi:mi + 1], r, AX.X, ALU.add), sem="V")
        wait("tensor", "V", C["V"])
        op("tensor", lambda q, mi=mi: q.matmul(psB[0].ap()[0:1, mi:mi + 1],
                                               wsraw.ap()[:, mi:mi + 1],
                                               onescol.ap()[:], start=True, stop=True), sem="TE")
        te_settle_s()
        wait("scalar", "TE", C["TE"])
        op("scalar", lambda q, mi=mi: q.copy(totsb.ap()[0:1, mi:mi + 1],
                                             psB[0].ap()[0:1, mi:mi + 1]), sem="S")
        wait("tensor", "S", C["S"])
        op("tensor", lambda q, mi=mi: q.matmul(psB[1].ap()[0:128, mi:mi + 1],
                                               onesrow.ap()[:],
                                               totsb.ap()[0:1, mi:mi + 1],
                                               start=True, stop=True), sem="TE")
        te_settle_s()
        wait("scalar", "TE", C["TE"])
        op("scalar", lambda q, mi=mi: q.copy(wsraw.ap()[:, mi:mi + 1],
                                             psB[1].ap()[0:128, mi:mi + 1]), sem="S")
        wait("vector", "S", C["S"])
        op("vector", lambda q, mi=mi: q.tensor_scalar(wsb.ap()[:, mi:mi + 1],
                                                      wsraw.ap()[:, mi:mi + 1],
                                                      INV2_24, EPS, ALU.mult, ALU.add))
        recip(lambda q, mi=mi: q.reciprocal(rwsb.ap()[:, mi:mi + 1],
                                            wsb.ap()[:, mi:mi + 1]), "rws")
    V_WS = C["V"]
    mark()

    xt_use = [0, 0]
    xt_cons = [None, None]

    def xt_load(src_ap, slot, extra_wait=None):
        if xt_cons[slot] is not None:
            nm, vv = xt_cons[slot]
            wait("scalar", nm, vv)
        if extra_wait is not None:
            wait("scalar", extra_wait[0], extra_wait[1])
        dma("scalar", xt.ap()[:, slot * D:(slot + 1) * D], src_ap, f"XT{slot}")

    xq_te = [0, 0]

    def phase_L(src_h, do_quant_stats=True):
        for t in range(TT):
            slot = t % 2
            xt_load(src_h[t * P:(t + 1) * P, :], slot)
            wait("vector", f"XT{slot}", C[f"XT{slot}"])
            xv = xt.ap()[:, slot * D:(slot + 1) * D]
            op("vector", lambda q, xv=xv, t=t:
               q.tensor_reduce(sumc.ap()[:, t:t + 1], xv, AX.X, ALU.add))
            if do_quant_stats:
                op("vector", lambda q, xv=xv, t=t:
                   q.tensor_reduce(xmaxc.ap()[:, t:t + 1], xv, AX.X, ALU.max))
                op("vector", lambda q, xv=xv, t=t:
                   q.tensor_reduce(xminc.ap()[:, t:t + 1], xv, AX.X, ALU.min))
            dr()
            op("vector", lambda q, t=t:
               q.tensor_scalar(mcol.ap()[:, t:t + 1], sumc.ap()[:, t:t + 1],
                               INV2048, None, ALU.mult))
            dr()
            op("vector", lambda q, t=t:
               q.tensor_scalar(nmcol.ap()[:, t:t + 1], mcol.ap()[:, t:t + 1],
                               -1.0, None, ALU.mult), sem="V")
            wait("scalar", "V", C["V"])
            op("scalar", lambda q, xv=xv, t=t:
               q.activation(sqscr, xv, AF.Square, bias=nmcol.ap()[:, t:t + 1],
                            scale=1.0, accum_out=ssqc.ap()[:, t:t + 1]), sem="S")
            xt_cons[slot] = ("S", C["S"])
        wait("vector", "S", C["S"])
        op("vector", lambda q: q.tensor_scalar(vc[:], ssqc[:], INV2048, EPS,
                                               ALU.mult, ALU.add), sem="V")
        wait("scalar", "V", C["V"])
        op("scalar", lambda q: q.activation(sdc[:], vc[:], AF.Sqrt), sem="S")
        wait("vector", "S", C["S"])
        recip(lambda q: q.reciprocal(rstdc[:], sdc[:]), "rstd")
        if do_quant_stats:
            dr()
            op("vector", lambda q: q.tensor_tensor(amax1[:], xmaxc[:], mcol[:], ALU.subtract))
            op("vector", lambda q: q.tensor_tensor(amax2[:], mcol[:], xminc[:], ALU.subtract))
            dr()
            op("vector", lambda q: q.tensor_tensor(amaxc[:], amax1[:], amax2[:], ALU.max))
            dr()
            op("vector", lambda q: q.tensor_tensor(tmp1c[:], amaxc[:], rstdc[:], ALU.mult))
            dr()
            op("vector", lambda q: q.tensor_scalar(sc_[:], tmp1c[:], INV127, EPS,
                                                   ALU.mult, ALU.add))
            recip(lambda q: q.reciprocal(rsc[:], sc_[:]), "rsc")
            op("vector", lambda q: q.tensor_tensor(qqc[:], rstdc[:], rsc[:], ALU.mult))
            dr()
            op("vector", lambda q: q.tensor_tensor(nmqc[:], nmcol[:], qqc[:], ALU.mult),
               sem="V")
        else:
            op("vector", lambda q: q.tensor_copy(tmp1c.ap()[:, 0:1], rstdc.ap()[:, 0:1]),
               sem="V")
        V_STATS = C["V"]

        if not do_quant_stats:
            return

        for t in range(TT):
            slot = t % 2
            xt_load(src_h[t * P:(t + 1) * P, :], slot)
            wait("scalar", f"XT{slot}", C[f"XT{slot}"])
            wait("scalar", "V", V_STATS)
            pp = t % 2
            wait("scalar", f"CX{pp}", C[f"CX{pp}"])
            xv = xt.ap()[:, slot * D:(slot + 1) * D]
            op("scalar", lambda q, xv=xv, t=t, pp=pp:
               q.activation(xqi8.ap()[:, pp * D:(pp + 1) * D], xv, AF.Identity,
                            bias=nmqc.ap()[:, t:t + 1], scale=qqc.ap()[:, t:t + 1]), sem="S")
            xt_cons[slot] = ("S", C["S"])
            wait("gpsimd", "S", C["S"])
            wait("gpsimd", "TE", xq_te[pp])
            dma("gpsimd", xqb.ap()[:, pp * D:(pp + 1) * D],
                xqi8.ap()[:, pp * D:(pp + 1) * D], f"CX{pp}")
            wait("tensor", f"CX{pp}", C[f"CX{pp}"])
            if psB_tr[t % 2] is None:
                wait("tensor", "V", C["V"])
                wait("tensor", "S", C["S"])
            else:
                wait("tensor", "V", psB_tr[t % 2])
            for kk in range(KD):
                bank = (t % 2) * 2 + kk // 8
                sl = kk % 8
                op("tensor", lambda q, pp=pp, kk=kk, bank=bank, sl=sl:
                   q.transpose(psB3[bank][:, sl, :],
                               xqb.ap()[:, pp * D + kk * P: pp * D + (kk + 1) * P],
                               identb.ap()[:]), sem=("TE" if kk % 8 == 7 else None))
            xq_te[pp] = C["TE"]
            te_settle_v()
            wait("vector", "TE", C["TE"])
            for half in range(2):
                bank = (t % 2) * 2 + half
                op("vector", lambda q, bank=bank, t=t, half=half:
                   q.tensor_copy(xqT3[:, half * 8:(half + 1) * 8, t * P:(t + 1) * P],
                                 psB3[bank][:, :, :]), sem=("V" if half == 1 else None))
            psB_tr[t % 2] = C["V"]

    pipe_idx = [0]
    wq_te = [0, 0]
    pipe_v_slab = {}
    slab_guard = {}
    dstg_guard = [0, 0]
    wtmp_cons = [0, 0]
    psB_tr = [None, None]
    psA_tr = [None, None]

    def pipe_piece(mn, w_src_ap, dest_kind, dest, r, mi):
        gi = pipe_idx[0]; pipe_idx[0] += 1
        slot = gi % 2
        wtile_load(w_src_ap, slot, q=("sync" if slot == 0 else "scalar"))
        wait("scalar", f"LDW{slot}", C[f"LDW{slot}"])
        wait("scalar", "V", wtmp_cons[gi % 2])
        wt = wtmp[gi % 2]
        op("scalar", lambda q, sl=slot, wt=wt, mi=mi:
           q.activation(wt, wtile.ap()[:, sl * D:(sl + 1) * D], AF.Identity,
                        bias=0.0, scale=rwsb.ap()[:, mi:mi + 1]), sem="S")
        wtile_cons[slot] = ("S", C["S"])
        pp = gi % 2
        wait("vector", "S", C["S"])
        wait("vector", f"CW{pp}", C[f"CW{pp}"])
        op("vector", lambda q, wt=wt, pp=pp:
           q.tensor_scalar(wqi8.ap()[:, pp * D:(pp + 1) * D], wt, 1.0, -1.0,
                           ALU.min, ALU.max), sem="V")
        wtmp_cons[gi % 2] = C["V"]
        wait("gpsimd", "V", C["V"])
        wait("gpsimd", "TE", wq_te[pp])
        dma("gpsimd", wqb_v[:, pp * D:(pp + 1) * D],
            wqi8.ap()[:, pp * D:(pp + 1) * D], f"CW{pp}")
        wait("tensor", f"CW{pp}", C[f"CW{pp}"])
        wait("tensor", "V", C["V"] if psB_tr[gi % 2] is None else psB_tr[gi % 2])
        for kk in range(KD):
            bank = (gi % 2) * 2 + kk // 8
            sl = kk % 8
            op("tensor", lambda q, pp=pp, kk=kk, bank=bank, sl=sl:
               q.transpose(psB3[bank][:, sl, :],
                           wqb_v[:, pp * D + kk * P: pp * D + (kk + 1) * P],
                           identb.ap()[:]), sem=("TE" if kk % 8 == 7 else None))
        wq_te[pp] = C["TE"]
        te_settle_v()
        wait("vector", "TE", C["TE"])
        if dest_kind == "slab":
            sloti = dest
            if r == 0 and sloti in slab_guard:
                te_v, spa_n, spa_v = slab_guard[sloti]
                wait("vector", "TE", max(te_v, C["TE"]))
                wait("vector", spa_n, spa_v)
            for half in range(2):
                bank = (gi % 2) * 2 + half
                op("vector", lambda q, bank=bank, sloti=sloti, r=r, half=half:
                   q.tensor_copy(slab3[sloti][:, half * 8:(half + 1) * 8, r * P:(r + 1) * P],
                                 psB3[bank][:, :, :]), sem=("V" if half == 1 else None))
            psB_tr[gi % 2] = C["V"]
        else:
            j, qq = dest
            dsl = gi % 2
            nm = f"SPD{dsl}"
            wait("vector", nm, C[nm])
            d3 = dstg_v[:, dsl * D:(dsl + 1) * D].rearrange("p (e c) -> p e c", e=16)
            for half in range(2):
                bank = (gi % 2) * 2 + half
                op("vector", lambda q, bank=bank, d3=d3, half=half:
                   q.tensor_copy(d3[:, half * 8:(half + 1) * 8, :], psB3[bank][:, :, :]),
                   sem=("V" if half == 1 else None))
            psB_tr[gi % 2] = C["V"]
            wait("sync", "V", C["V"])
            for k2 in range(16):
                dma("sync", RdT[:, qq * 16 + k2, j * P:(j + 1) * P],
                    d3[:, k2, :], nm)

    psA_free = [("S", 0, "V", 0)] * 2
    psB_free_y = [("V", 0)] * 4

    def ffn(src_h, resid_h, dst_h, produce, tag):
        psB_tr[0] = psB_tr[1] = None
        psA_tr[0] = psA_tr[1] = None
        phase_L(src_h, do_quant_stats=True)
        mark()
        op("vector", lambda q: q.tensor_scalar(sgat[:], sc_[:], wsb.ap()[:, 0:1], None, ALU.mult))
        op("vector", lambda q: q.tensor_scalar(supc[:], sc_[:], wsb.ap()[:, 1:2], None, ALU.mult))
        op("vector", lambda q: q.memset(hmaxc[:], 0.0), sem="V")
        dr()

        for oc in range(OCA):
            for mi, mn in ((0, "g"), (1, "u")):
                slotd = mi * 2 + oc % 2
                if produce:
                    w_ap = wg_in if mn == "g" else wu_in
                    for r in range(4):
                        j = oc * 4 + r
                        pipe_piece(mn, w_ap[j * P:(j + 1) * P, :], "slab", slotd, r, mi)
                    pipe_v_slab[(mn, oc)] = C["V"]
                    Rt = RgT if mn == "g" else RuT
                    spa = f"SPA{oc % 2}"
                    wait("sync", "V", C["V"])
                    for k2 in range(KD):
                        dma("sync", Rt[:, k2, oc * 512:(oc + 1) * 512],
                            slab3[slotd][:, k2, :], spa)
                else:
                    Rt = RgT if mn == "g" else RuT
                    sl_sem = f"SL{slotd}"
                    if (tag, slotd) in slab_guard:
                        te_v = slab_guard[(tag, slotd)]
                        wait("sync", "TE", te_v)
                    for k2 in range(KD):
                        dma("sync", slab3[slotd][:, k2, :],
                            Rt[:, k2, oc * 512:(oc + 1) * 512], sl_sem)
            if produce:
                for r in range(4):
                    pi = oc * 4 + r
                    j = pi // 4
                    qq = pi % 4
                    pipe_piece("d", wd_in[j * P:(j + 1) * P, qq * D:(qq + 1) * D],
                               "dstage", (j, qq), 0, 2)

            slot_g = 0 + oc % 2
            slot_u = 2 + oc % 2
            if produce:
                wait("tensor", "V", max(pipe_v_slab[("g", oc)], pipe_v_slab[("u", oc)]))
            else:
                wait("tensor", f"SL{slot_g}", C[f"SL{slot_g}"])
                wait("tensor", f"SL{slot_u}", C[f"SL{slot_u}"])
            for t in range(TT):
                pr = t % 2
                sN, sV_, vN, vV_ = psA_free[pr]
                wait("tensor", vN, vV_)
                wait("tensor", sN, sV_)
                pg = psA[pr * 2].ap()
                pu = psA[pr * 2 + 1].ap()
                for k in range(KD):
                    st = (k == 0); sp = (k == KD - 1)
                    op("tensor", lambda q, k=k, t=t, pg=pg, sg=slot_g, st=st, sp=sp:
                       q.matmul(pg[:, 0:512], xqT3[:, k, t * P:(t + 1) * P],
                                slab3[sg][:, k, :], start=st, stop=sp))
                    op("tensor", lambda q, k=k, t=t, pu=pu, su=slot_u, st=st, sp=sp:
                       q.matmul(pu[:, 0:512], xqT3[:, k, t * P:(t + 1) * P],
                                slab3[su][:, k, :], start=st, stop=sp),
                       sem=("TE" if sp else None))
                mm_done = C["TE"]
                eh = (oc * TT + t) % 2
                wait("scalar", "TE", mm_done)
                wait("scalar", "V", psA_free[pr][3])
                op("scalar", lambda q, pg=pg, t=t, eh=eh:
                   q.activation(hgb.ap()[:, eh * 512:(eh + 1) * 512], pg[:, 0:512],
                                AF.Silu, bias=0.0, scale=sgat.ap()[:, t:t + 1]), sem="S")
                s_done = C["S"]
                wait("vector", "S", s_done)
                wait("vector", f"HS{eh}", C[f"HS{eh}"])
                op("vector", lambda q, pu=pu, t=t, eh=eh:
                   q.scalar_tensor_tensor(hob.ap()[:, eh * 512:(eh + 1) * 512], pu[:, 0:512],
                                          supc.ap()[:, t:t + 1],
                                          hgb.ap()[:, eh * 512:(eh + 1) * 512],
                                          ALU.mult, ALU.mult))
                dr()
                op("vector", lambda q, eh=eh:
                   q.tensor_reduce(tmp1c.ap()[:, 0:1], hob.ap()[:, eh * 512:(eh + 1) * 512],
                                   AX.X, ALU.max, apply_absolute_value=True))
                dr()
                op("vector", lambda q, t=t:
                   q.tensor_tensor(hmaxc.ap()[:, t:t + 1], hmaxc.ap()[:, t:t + 1],
                                   tmp1c.ap()[:, 0:1], ALU.max), sem="V")
                v_done = C["V"]
                psA_free[pr] = ("S", s_done, "V", v_done)
                wait("scalar", "V", v_done)
                dma("scalar", h_h[t * P:(t + 1) * P, oc * 512:(oc + 1) * 512],
                    hob.ap()[:, eh * 512:(eh + 1) * 512], f"HS{eh}")
            if not produce:
                slab_guard[(tag, slot_g)] = C["TE"]
                slab_guard[(tag, slot_u)] = C["TE"]
            else:
                slab_guard[slot_g] = (C["TE"], f"SPA{oc % 2}", C[f"SPA{oc % 2}"])
                slab_guard[slot_u] = (C["TE"], f"SPA{oc % 2}", C[f"SPA{oc % 2}"])

        mark()
        wait("vector", "V", C["V"])
        dr()
        op("vector", lambda q: q.tensor_scalar(s2c[:], hmaxc[:], INV127, EPS,
                                               ALU.mult, ALU.add))
        recip(lambda q: q.reciprocal(rs2c[:], s2c[:]), "rs2")
        op("vector", lambda q: q.tensor_scalar(scoc[:], s2c[:], wsb.ap()[:, 2:3], 0.5,
                                               ALU.mult, ALU.mult), sem="V")
        dr()
        V_H = C["V"]

        HS_ALL0, HS_ALL1 = C["HS0"], C["HS1"]

        for half in range(2):
            if half == 1 and BARRIER:
                for e in ("scalar", "vector", "tensor", "gpsimd", "sync"):
                    wait(e, "TE", C["TE"])
                    wait(e, "V", C["V"])
                    wait(e, "S", C["S"])
                    wait(e, "YW0", C["YW0"])
                    wait(e, "YW1", C["YW1"])
            for tl in range(4):
                t = half * 4 + tl
                for hc in range(4):
                    ci = (t * 4 + hc) % 2
                    slot = ci
                    if wtile_cons[slot] is not None:
                        nm, vv = wtile_cons[slot]
                        wait("scalar", nm, vv)
                    if half == 0 and tl == 0 and hc == 0:
                        wait("scalar", "HS0", HS_ALL0)
                        wait("scalar", "HS1", HS_ALL1)
                    dma("scalar", wtile.ap()[:, slot * D:(slot + 1) * D],
                        h_h[t * P:(t + 1) * P, hc * D:(hc + 1) * D], f"LDW{slot}")
                    wait("scalar", f"LDW{slot}", C[f"LDW{slot}"])
                    wait("scalar", "V", V_H)
                    pp = ci
                    wait("scalar", f"CX{pp}", C[f"CX{pp}"])
                    op("scalar", lambda q, sl=slot, t=t, pp=pp:
                       q.activation(xqi8.ap()[:, pp * D:(pp + 1) * D],
                                    wtile.ap()[:, sl * D:(sl + 1) * D], AF.Identity,
                                    bias=0.0, scale=rs2c.ap()[:, t:t + 1]), sem="S")
                    wtile_cons[slot] = ("S", C["S"])
                    wait("gpsimd", "S", C["S"])
                    wait("gpsimd", "TE", xq_te[pp])
                    dma("gpsimd", xqb.ap()[:, pp * D:(pp + 1) * D],
                        xqi8.ap()[:, pp * D:(pp + 1) * D], f"CX{pp}")
                    wait("tensor", f"CX{pp}", C[f"CX{pp}"])
                    wait("tensor", "V",
                         C["V"] if psA_tr[ci] is None else psA_tr[ci])
                    for kk in range(KD):
                        bank = ci * 2 + kk // 8
                        sl2 = kk % 8
                        op("tensor", lambda q, pp=pp, kk=kk, bank=bank, sl2=sl2:
                           q.transpose(psA3[bank][:, sl2, :],
                                       xqb.ap()[:, pp * D + kk * P: pp * D + (kk + 1) * P],
                                       identb.ap()[:]), sem=("TE" if kk % 8 == 7 else None))
                    xq_te[pp] = C["TE"]
                    te_settle_v()
                    wait("vector", "TE", C["TE"])
                    for hh in range(2):
                        bank = ci * 2 + hh
                        kbase = hc * 16 + hh * 8
                        op("vector", lambda q, bank=bank, kbase=kbase, tl=tl:
                           q.tensor_copy(hqT3[:, kbase:kbase + 8, tl * P:(tl + 1) * P],
                                         psA3[bank][:, :, :]), sem=("V" if hh == 1 else None))
                    psA_tr[ci] = C["V"]
            V_HQ = C["V"]
            mark()
            for oc in range(OCB):
                for kh in range(2):
                    Q = "sync" if kh == 0 else "gpsimd"
                    nm = f"WD{kh}"
                    if ("wd", kh) in slab_guard:
                        wait(Q, "TE", slab_guard[("wd", kh)])
                    if oc == 0 and half == 0:
                        wait(Q, "TE", C["TE"])
                        if kh == 0:
                            wait(Q, "S", C["S"])
                            wait(Q, "V", C["V"])
                            wait(Q, "SPA0", C["SPA0"])
                            wait(Q, "SPA1", C["SPA1"])
                            if produce:
                                wait(Q, "SPD0", C["SPD0"])
                                wait(Q, "SPD1", C["SPD1"])
                    for k2 in range(32):
                        dma(Q, wdsl3[kh][:, k2, :],
                            RdT[:, kh * 32 + k2, oc * 256:(oc + 1) * 256], nm)
                wait("tensor", "V", V_HQ)
                for kh in range(2):
                    wait("tensor", f"WD{kh}", C[f"WD{kh}"])
                    for tl in range(4):
                        if kh == 0:
                            nmv, vv = psB_free_y[tl]
                            wait("tensor", nmv, vv)
                        for kk in range(32):
                            k = kh * 32 + kk
                            st = (k == 0); sp = (k == KH - 1)
                            op("tensor", lambda q, tl=tl, k=k, kk=kk, kh=kh, st=st, sp=sp:
                               q.matmul(psB[tl].ap()[:, 0:256],
                                        hqT3[:, k, tl * P:(tl + 1) * P],
                                        wdsl3[kh][:, kk, :], start=st, stop=sp),
                               sem=("TE" if (sp or kk == 31) else None))
                    slab_guard[("wd", kh)] = C["TE"]
                for tl in range(4):
                    t = half * 4 + tl
                    eh = (oc * 4 + tl) % 2
                    mm_done = C["TE"] - (3 - tl)
                    wait("scalar", "TE", mm_done)
                    wait("scalar", f"RS{eh}", C[f"RS{eh}"])
                    wait("scalar", "V", C["V"])
                    dma("scalar", rsb.ap()[:, eh * 512:eh * 512 + 256],
                        resid_h[t * P:(t + 1) * P, oc * 256:(oc + 1) * 256], f"RS{eh}")
                    wait("vector", f"RS{eh}", C[f"RS{eh}"])
                    wait("vector", "TE", mm_done)
                    wait("vector", f"YW{eh}", C[f"YW{eh}"])
                    op("vector", lambda q, tl=tl, t=t, eh=eh:
                       q.scalar_tensor_tensor(yob.ap()[:, eh * 512:eh * 512 + 256],
                                              psB[tl].ap()[:, 0:256],
                                              scoc.ap()[:, t:t + 1],
                                              rsb.ap()[:, eh * 512:eh * 512 + 256],
                                              ALU.mult, ALU.add), sem="V")
                    psB_free_y[tl] = ("V", C["V"])
                    wait("scalar", "V", C["V"])
                    dma("scalar", dst_h[t * P:(t + 1) * P, oc * 256:(oc + 1) * 256],
                        yob.ap()[:, eh * 512:eh * 512 + 256], f"YW{eh}")
                mark()
        mark("ffn_end")

    def attn():
        wait("scalar", "YW0", C["YW0"])
        wait("scalar", "YW1", C["YW1"])
        wait("sync", "TE", C["TE"])
        dma("sync", wcmv[0:64, :], wcm_in, "SM")
        dma("sync", dbc[0:1, :], dv_in, "SM")
        SM_W = C["SM"]
        phase_L(x1_h, do_quant_stats=False)
        V_ST = C["V"]
        mark()
        wait("tensor", "SM", SM_W)
        for ocq in range(4):
            bank = ocq % 2
            wait("tensor", "S", C["S"])
            wait("tensor", "V", C["V"])
            op("tensor", lambda q, ocq=ocq, bank=bank:
               q.matmul(psB[bank].ap()[:, 0:512], onesrow.ap()[:],
                        dbc[0:1, ocq * 512:(ocq + 1) * 512],
                        start=True, stop=True), sem="TE")
            te_settle_s()
            wait("scalar", "TE", C["TE"])
            op("scalar", lambda q, ocq=ocq, bank=bank:
               q.copy(dbc[:, ocq * 512:(ocq + 1) * 512], psB[bank].ap()[:, 0:512]), sem="S")
        mark()
        for t in range(TT):
            slot = t % 2
            xt_load(x1_h[t * P:(t + 1) * P, :], slot)
            wait("vector", f"XT{slot}", C[f"XT{slot}"])
            wait("vector", "V", V_ST)
            wait("vector", "S", C["S"])
            op("vector", lambda q, sl=slot, t=t:
               q.tensor_scalar(sqscr, xt.ap()[:, sl * D:(sl + 1) * D],
                               mcol.ap()[:, t:t + 1], None, ALU.subtract), sem="V")
            xt_cons[slot] = ("V", C["V"])
            wait("tensor", "V", C["V"])
            for kk in range(KD):
                bank = kk // 4
                sl2 = kk % 4
                op("tensor", lambda q, kk=kk, bank=bank, sl2=sl2:
                   q.transpose(psAf3[bank][:, sl2, :], sqscr[:, kk * P:(kk + 1) * P],
                               identf.ap()[:]), sem=("TE" if kk == KD - 1 else None))
            te_settle_v()
            wait("vector", "TE", C["TE"])
            for bank in range(4):
                op("vector", lambda q, bank=bank, t=t:
                   q.tensor_copy(xcT3[:, bank * 4:(bank + 1) * 4, t * P:(t + 1) * P],
                                 psAf3[bank][:, :, :]), sem=("V" if bank == 3 else None))
        mark()
        wait("tensor", "SM", SM_W)
        for tc in range(2):
            wait("tensor", "V", C["V"])
            wait("tensor", "S", C["S"])
            for k in range(KD):
                st = (k == 0); sp = (k == KD - 1)
                op("tensor", lambda q, k=k, tc=tc, st=st, sp=sp:
                   q.matmul(psA[2].ap()[0:64, 0:512], wleft.ap()[:, k * 64:(k + 1) * 64],
                            xcT3[:, k, tc * 512:(tc + 1) * 512], start=st, stop=sp),
                   sem=("TE" if sp else None))
            te_settle_s()
            wait("scalar", "TE", C["TE"])
            op("scalar", lambda q, tc=tc:
               q.copy(t1Tv[0:64, tc * 512:(tc + 1) * 512], psA[2].ap()[0:64, 0:512]), sem="S")
        S_T1T = C["S"]
        mark()
        y_guard = {2: ("V", C["V"]), 3: ("V", C["V"])}
        for t in range(TT):
            for ocq in range(4):
                bank = 2 + (t * 4 + ocq) % 2
                wait("tensor", "S", S_T1T)
                nmv, vv = y_guard[bank]
                wait("tensor", nmv, vv)
                op("tensor", lambda q, t=t, ocq=ocq, bank=bank:
                   q.matmul(psA[bank].ap()[:, 0:512], t1Tv[0:64, t * P:(t + 1) * P],
                            wcmv[0:64, ocq * 512:(ocq + 1) * 512],
                            start=True, stop=True), sem="TE")
                mm_done = C["TE"]
                eh = (t * 4 + ocq) % 2
                wait("scalar", f"RS{eh}", C[f"RS{eh}"])
                wait("scalar", "V", C["V"])
                dma("scalar", rsb.ap()[:, eh * 512:(eh + 1) * 512],
                    x1_h[t * P:(t + 1) * P, ocq * 512:(ocq + 1) * 512], f"RS{eh}")
                wait("vector", f"RS{eh}", C[f"RS{eh}"])
                wait("vector", "TE", mm_done)
                wait("vector", f"YW{eh}", C[f"YW{eh}"])
                op("vector", lambda q, t=t, eh=eh, ocq=ocq:
                   q.scalar_tensor_tensor(hgb.ap()[:, eh * 512:(eh + 1) * 512],
                                          rsb.ap()[:, eh * 512:(eh + 1) * 512],
                                          mcol.ap()[:, t:t + 1],
                                          dbc[:, ocq * 512:(ocq + 1) * 512],
                                          ALU.subtract, ALU.mult))
                dr()
                op("vector", lambda q, eh=eh, bank=bank:
                   q.tensor_tensor(hob.ap()[:, eh * 512:(eh + 1) * 512],
                                   hgb.ap()[:, eh * 512:(eh + 1) * 512],
                                   psA[bank].ap()[:, 0:512], ALU.add))
                dr()
                op("vector", lambda q, t=t, eh=eh:
                   q.scalar_tensor_tensor(yob.ap()[:, eh * 512:(eh + 1) * 512],
                                          hob.ap()[:, eh * 512:(eh + 1) * 512],
                                          rstdc.ap()[:, t:t + 1],
                                          rsb.ap()[:, eh * 512:(eh + 1) * 512],
                                          ALU.mult, ALU.add), sem="V")
                y_guard[bank] = ("V", C["V"])
                wait("scalar", "V", C["V"])
                dma("scalar", x2_h[t * P:(t + 1) * P, ocq * 512:(ocq + 1) * 512],
                    yob.ap()[:, eh * 512:(eh + 1) * 512], f"YW{eh}")

    ffn(u_in, u_in, x1_h, produce=True, tag=1)
    attn()
    mark("attn_end")
    wait("scalar", "YW0", C["YW0"])
    wait("scalar", "YW1", C["YW1"])
    wait("sync", "SPA0", C["SPA0"])
    wait("sync", "SPA1", C["SPA1"])
    wait("sync", "SPD0", C["SPD0"])
    wait("sync", "SPD1", C["SPD1"])
    wait("sync", "TE", C["TE"])
    wait("sync", "V", C["V"])
    ffn(x2_h, x2_h, out_d, produce=False, tag=2)

    wait("gpsimd", "YW0", C["YW0"])
    wait("gpsimd", "YW1", C["YW1"])
    mark()
    build.last_marks = marks

    if isinstance(phases, str):
        if phases == "ffn1_end":
            phases = [i for i, m in enumerate(marks) if m[2] == "ffn_end"][0] + 1
        elif phases == "ffn2_end":
            phases = [i for i, m in enumerate(marks) if m[2] == "ffn_end"][1] + 1
        elif phases == "attn_end":
            phases = [i for i, m in enumerate(marks) if m[2] == "attn_end"][0] + 1
    if phases is not None and phases < len(marks):
        cut, cutC, _lbl = marks[phases]
        for k in prog:
            prog[k] = prog[k][:cut[k]]
    else:
        cutC = dict(C)
    if dbg is not None:
        for nm, vv in cutC.items():
            p("gpsimd", lambda q, s_=S[nm], v_=vv: q.wait_ge(s_, v_))
        ndma = 0
        if dbg == "arena0":
            for c2 in range(8):
                p("gpsimd", lambda q, c2=c2: q.dma_start(
                    out=out_d[c2 * 128:(c2 + 1) * 128, :],
                    in_=AFp[:, c2 * 2048:(c2 + 1) * 2048]).then_inc(S["SM"], 16))
                ndma += 1
        elif dbg == "arena1":
            for c2 in range(8):
                p("gpsimd", lambda q, c2=c2: q.dma_start(
                    out=out_d[c2 * 128:(c2 + 1) * 128, :],
                    in_=AFp[:, 16384 + c2 * 2048:16384 + (c2 + 1) * 2048]).then_inc(S["SM"], 16))
                ndma += 1
        elif dbg in ("x1", "x2", "hq0"):
            srcT = {"x1": x1_h, "x2": x2_h}.get(dbg)
            if dbg == "hq0":
                srcT = h_h[:, 0:2048]
            for c2 in range(8):
                p("gpsimd", lambda q, c2=c2, srcT=srcT: q.dma_start(
                    out=out_d[c2 * 128:(c2 + 1) * 128, :],
                    in_=srcT[c2 * 128:(c2 + 1) * 128, :]).then_inc(S["SM"], 16))
                ndma += 1
        elif dbg == "wgT":
            bt = wgT_h.bitcast(F32)
            for c2 in range(8):
                p("gpsimd", lambda q, c2=c2, bt=bt: q.dma_start(
                    out=out_d[c2 * 128:(c2 + 1) * 128, :],
                    in_=bt[c2 * 128:(c2 + 1) * 128, 0:2048]).then_inc(S["SM"], 16))
                ndma += 1
        elif dbg == "stats":
            dumps = [wsb, rwsb, sc_, rsc, rstdc, qqc, nmqc, sgat, supc, hmaxc,
                     s2c, rs2c, scoc, mcol, ssqc, vc, sdc, wsraw, amaxc, sumc]
            for i2, tens in enumerate(dumps):
                w2 = tens.ap().shape[1]
                p("gpsimd", lambda q, i2=i2, tens=tens, w2=w2: q.dma_start(
                    out=out_d[0:128, i2 * 8:i2 * 8 + w2],
                    in_=tens.ap()[:, 0:w2]).then_inc(S["SM"], 16))
                ndma += 1
        elif dbg == "wdT":
            bt = wdT_h.bitcast(F32)
            for c2 in range(8):
                p("gpsimd", lambda q, c2=c2, bt=bt: q.dma_start(
                    out=out_d[c2 * 128:(c2 + 1) * 128, 0:1024],
                    in_=bt[c2 * 128:(c2 + 1) * 128, 0:1024]).then_inc(S["SM"], 16))
                ndma += 1
        tot = cutC["SM"] + 16 * ndma
        p("gpsimd", lambda q, tot=tot: q.wait_ge(S["SM"], tot))

    with es, nc.Block() as blk:
        @blk.sync
        def _(q):
            for f in prog["sync"]:
                f(q)

        @blk.scalar
        def _(q):
            for f in prog["scalar"]:
                f(q)

        @blk.vector
        def _(q):
            for f in prog["vector"]:
                f(q)

        @blk.tensor
        def _(q):
            for f in prog["tensor"]:
                f(q)

        @blk.gpsimd
        def _(q):
            for f in prog["gpsimd"]:
                f(q)
    return nc


_NC_CACHE = []


def kernel(L0_attn_u, L0_attn_A_factors, L0_attn_W_b_rank, L0_attn_W_b_state,
           L0_attn_W_c_rank, L0_attn_W_c_model, L0_attn_D_vec,
           L0_ffn_w_gate, L0_ffn_w_up, L0_ffn_w_down):
    u = np.ascontiguousarray(np.asarray(L0_attn_u, dtype=np.float32).reshape(8192, 2048))
    wg = np.ascontiguousarray(np.asarray(L0_ffn_w_gate, dtype=np.float32))
    wu = np.ascontiguousarray(np.asarray(L0_ffn_w_up, dtype=np.float32))
    wd = np.ascontiguousarray(np.asarray(L0_ffn_w_down, dtype=np.float32))
    a_fl = np.ascontiguousarray(np.asarray(L0_attn_A_factors, dtype=np.float32).reshape(1, 32))
    wbr = np.ascontiguousarray(np.asarray(L0_attn_W_b_rank, dtype=np.float32))
    wbs = np.ascontiguousarray(np.asarray(L0_attn_W_b_state, dtype=np.float32))
    wcr = np.ascontiguousarray(np.asarray(L0_attn_W_c_rank, dtype=np.float32))
    wcm = np.ascontiguousarray(np.asarray(L0_attn_W_c_model, dtype=np.float32))
    dv = np.ascontiguousarray(np.asarray(L0_attn_D_vec, dtype=np.float32).reshape(1, 2048))
    idb = np.eye(P, dtype=np.float32).astype(ml_dtypes.bfloat16)
    idf = np.eye(P, dtype=np.float32)
    onc = np.ones((P, 1), dtype=np.float32)
    onr = np.ones((1, P), dtype=np.float32)

    if not _NC_CACHE:
        _NC_CACHE.append(build())
    nc = _NC_CACHE[0]

    in_maps = []
    for c in range(8):
        in_maps.append(dict(u=np.ascontiguousarray(u[c * TOK:(c + 1) * TOK]),
                            wg=wg, wu=wu, wd=wd, a_fl=a_fl, wbr=wbr, wbs=wbs,
                            wcr=wcr, wcm=wcm, dv=dv, idb=idb, idf=idf,
                            onc=onc, onr=onr))
    res = run_bass_kernel_spmd(nc, in_maps, core_ids=list(range(8)))
    out = np.concatenate([res.results[c]["out"] for c in range(8)], axis=0)
    return out.reshape(4, 2048, 2048)


if __name__ == "__main__":
    nc = build()
    print("build OK")
